# version 18
# baseline (speedup 1.0000x reference)
import contextlib
import numpy as np
import ml_dtypes
import concourse.bass as bass
import concourse.mybir as mybir
from concourse.bass_utils import run_bass_kernel_spmd

F32 = mybir.dt.float32
BF16 = mybir.dt.bfloat16
I32 = mybir.dt.int32
AF = mybir.ActivationFunctionType
ALU = mybir.AluOpType

B, S, D = 8, 2048, 1024
H, Hd = 16, 64
HID = 3072
SEG, PM = 512, 16
NT, ND, NHC, NSEG = S // 128, D // 128, HID // 128, S // SEG

BETA = 0.95
_a4 = 1 + BETA * (1 + BETA * (1 + BETA))
TH = (1.0 / _a4, (1 + BETA) / _a4, (2 + BETA) / _a4, (2 + BETA + BETA**2) / _a4)
DQ1 = 1.0 / 127.0
DQ2 = 1.0 / (127.0 * 127.0)


def _deint_perm():
    p = []
    for h in range(H):
        p += [h * 64 + 2 * i for i in range(32)]
        p += [h * 64 + 2 * i + 1 for i in range(32)]
    return np.array(p, dtype=np.int64)


def host_prep(ins):
    c = np.ascontiguousarray
    o = {}

    def lhsT_tiles(w):
        return c(w.reshape(ND, 128, ND, 128).transpose(2, 1, 0, 3).reshape(ND, 128, D))

    o["a_qw"] = lhsT_tiles(np.asarray(ins["L0_attn_a_q_w"], np.float32))
    o["a_kw"] = lhsT_tiles(np.asarray(ins["L0_attn_a_k_w"], np.float32))
    o["a_vw"] = c(np.asarray(ins["L0_attn_a_v_w"], np.float32).reshape(ND, 128, D))
    o["a_ow"] = c(np.asarray(ins["L0_attn_a_o_w"], np.float32).reshape(ND, 128, D))
    o["a_qb"] = c(np.asarray(ins["L0_attn_a_q_b"], np.float32).reshape(ND, 128).T)
    o["a_kb"] = c(np.asarray(ins["L0_attn_a_k_b"], np.float32).reshape(ND, 128).T)
    o["a_vb"] = c(np.asarray(ins["L0_attn_a_v_b"], np.float32).reshape(1, D))
    o["a_ob"] = c(np.asarray(ins["L0_attn_a_o_b"], np.float32).reshape(1, D))

    def ffn_lhsT(w):
        return c(w.reshape(NHC, 128, ND, 128).transpose(0, 3, 2, 1).reshape(NHC, 128, D))
    o["f_wg"] = ffn_lhsT(np.asarray(ins["L0_ffn_w_gate"], np.int32))
    o["f_wu"] = ffn_lhsT(np.asarray(ins["L0_ffn_w_up"], np.int32))
    o["f_wd"] = c(np.asarray(ins["L0_ffn_w_down"], np.int32).T.reshape(NHC, 128, D))

    perm = _deint_perm()
    wqkv = np.asarray(ins["L0_attn_b_w_qkv"], np.float32).copy()
    wqkv[:, 0:D] = wqkv[:, 0:D][:, perm]
    wqkv[:, D:2 * D] = wqkv[:, D:2 * D][:, perm]
    o["b_wqkv"] = c(wqkv.reshape(ND, 128, 6, 512).transpose(2, 0, 1, 3))
    o["b_nw"] = c(np.asarray(ins["L0_attn_b_norm_w"], np.float32).reshape(ND, 128).T)
    pmk = np.asarray(ins["L0_attn_b_pm_k"], np.float32)[:, :, perm[:64]]
    pmkT = pmk.transpose(2, 0, 1).reshape(64, H * PM)
    o["b_pmkT"] = c(np.concatenate([pmkT, pmkT], axis=0))
    pmv = np.asarray(ins["L0_attn_b_pm_v"], np.float32)
    pv = np.ones((PM, H, 65), np.float32)
    pv[:, :, 0:64] = pmv.transpose(1, 0, 2)
    o["b_pmv"] = c(pv.reshape(PM, H * 65))
    o["b_wout"] = c(np.asarray(ins["L0_attn_b_w_out"], np.float32).reshape(ND, 128, D))
    o["b_bout"] = c(np.asarray(ins["L0_attn_b_b_out"], np.float32).reshape(1, D))

    o["c_ident"] = np.eye(128, dtype=ml_dtypes.bfloat16)
    m = np.zeros((128, NSEG * 512), np.float32)
    for kc in range(4):
        q = np.arange(512)[None, :]
        p = np.arange(128)[:, None]
        m[:, kc * 512:(kc + 1) * 512] = (q >= kc * 128 + p)
    o["c_mask"] = m.astype(ml_dtypes.bfloat16)
    inv = (1.0 / 10000.0 ** (np.arange(0, Hd, 2, dtype=np.float32) / Hd)).astype(np.float32)
    fr = np.arange(S, dtype=np.float32)[:, None] * inv[None, :]
    cos32, sin32 = np.cos(fr), np.sin(fr)
    cosh = np.concatenate([cos32, cos32], axis=1)
    sinh = np.concatenate([-sin32, sin32], axis=1)
    o["c_cossin"] = np.concatenate(
        [np.tile(cosh, (1, H)), np.tile(sinh, (1, H))], axis=1).astype(ml_dtypes.bfloat16)
    return o


class Sem:
    def __init__(self, stack, nc, name):
        self.h = stack.enter_context(nc.semaphore(name))
        self.n = 0


class Phase:
    ENGS = ("sync", "gpsimd", "tensor", "vector", "scalar")

    def __init__(self, nc, name):
        self.nc, self.name = nc, name
        self.ops = {e: [] for e in self.ENGS}

    def on(self, eng, fn):
        self.ops[eng].append(fn)

    def emit(self):
        with self.nc.Block(self.name) as blk:
            for eng in self.ENGS:
                fns = self.ops[eng]
                if fns:
                    getattr(blk, eng)(lambda e, fns=fns: [f(e) for f in fns])


def build_program(n_sub=4, debug=False):
    nc = bass.Bass("TRN2", target_bir_lowering=False)
    gs = contextlib.ExitStack()

    dr = {}
    def din(name, shape, dt):
        dr[name] = nc.dram_tensor(name, shape, dt, kind="ExternalInput").ap()
    din("x", [S, D], F32)
    din("a_qw", [ND, 128, D], F32); din("a_kw", [ND, 128, D], F32)
    din("a_vw", [ND, 128, D], F32); din("a_ow", [ND, 128, D], F32)
    din("a_qb", [128, ND], F32); din("a_kb", [128, ND], F32)
    din("a_vb", [1, D], F32); din("a_ob", [1, D], F32)
    din("f_wg", [NHC, 128, D], I32); din("f_wu", [NHC, 128, D], I32)
    din("f_wd", [NHC, 128, D], I32)
    din("b_wqkv", [6, ND, 128, 512], F32)
    din("b_nw", [128, ND], F32)
    din("b_pmkT", [128, H * PM], F32)
    din("b_pmv", [PM, H * 65], F32)
    din("b_wout", [ND, 128, D], F32)
    din("b_bout", [1, D], F32)
    din("c_ident", [128, 128], BF16)
    din("c_mask", [128, NSEG * 512], BF16)
    din("c_cossin", [S, 2048], BF16)
    y = nc.dram_tensor("y", [S, D], F32, kind="ExternalOutput").ap()
    dbg = {}
    if debug:
        for i in range(1, 4):
            dbg[i] = nc.dram_tensor(f"dbg{i}", [S, D], F32, kind="ExternalOutput").ap()
        for nm, cols in (("d_xT", ND * 2048), ("d_q", ND * 2048), ("d_k", ND * 2048),
                         ("d_v", NT * 1040), ("d_ao", NT * 1024)):
            dbg[nm] = nc.dram_tensor(nm, [128, cols], F32, kind="ExternalOutput").ap()

    with gs:
        sX = Sem(gs, nc, "sX"); sC = Sem(gs, nc, "sC"); sG = Sem(gs, nc, "sG")
        sT = Sem(gs, nc, "sT"); sV = Sem(gs, nc, "sV"); sA = Sem(gs, nc, "sA")
        sO = Sem(gs, nc, "sO")

        def ga(name, shape, dt, side="left"):
            return gs.enter_context(nc.sbuf_tensor(name, shape, dt, side=side))

        X = ga("X", [128, NT * 1024], BF16)
        junk = ga("junk", [128, 1024], BF16)
        ident = ga("ident", [128, 128], BF16)
        onesr = ga("onesr", [1, 128], BF16)
        sums = ga("sums", [128, NT], F32)
        sumsq = ga("sumsq", [128, NT], F32)
        mcol = ga("mcol", [128, NT], F32)
        vcol = ga("vcol", [128, NT], F32)
        tcol = ga("tcol", [128, NT], F32)
        scol = ga("scol", [128, NT], F32)
        bcol = ga("bcol", [128, NT], F32)
        rcol = ga("rcol", [128, 8], F32)
        neg4 = ga("neg4", [128, 1], F32)
        aot = [ga(f"aot{i}", [128, 1024], BF16) for i in range(2)]
        pA = gs.enter_context(nc.psum_tensor("pA", [128, 2048], F32))
        pB = gs.enter_context(nc.psum_tensor("pB", [128, 2048], F32))

        Xt = lambda t: X[:, t * 1024:(t + 1) * 1024]

        ph = Phase(nc, "load")
        def ld_x(e):
            for hf in range(2):
                e.dma_start(
                    X[:, hf * 8192:(hf + 1) * 8192].rearrange("p (t d) -> p t d", t=8),
                    dr["x"][hf * 1024:(hf + 1) * 1024].rearrange(
                        "(t p) d -> p t d", p=128)).then_inc(sX.h, 16)
        ph.on("gpsimd", ld_x); sX.n += 32
        ph.on("sync", lambda e: e.dma_start(ident[:], dr["c_ident"]).then_inc(sC.h, 16))
        sC.n += 16
        IDENT_AT = sC.n
        ph.on("vector", lambda e: e.memset(onesr[:], 1.0))
        ph.on("vector", lambda e: e.memset(neg4[:], -4.0))
        ph.emit()

        dump_ctr = [0]
        def emit_dump(nm, src_sb):
            if not debug:
                return
            dump_ctr[0] += 1
            ph = Phase(nc, f"dump{dump_ctr[0]}_" + nm)
            tgt = sO.n + 16
            def d(e, tgt=tgt):
                e.dma_start(dbg[nm], src_sb[:]).then_inc(sO.h, 16)
                e.wait_ge(sO.h, tgt)
            ph.on("gpsimd", d); sO.n += 16
            ph.emit()

        def emit_ln_stats(ph, have_sums, rms_fold, have_sumsq=False):
            eps_mul, eps_add = (1.0 + 1e-6, 1e-11) if rms_fold else (1.0, 1e-5)
            def v1(e):
                if not have_sumsq:
                    for t in range(NT):
                        if not have_sums and t == 0:
                            e.wait_ge(sX.h, 16)
                        if not have_sums and t == 8:
                            e.wait_ge(sX.h, 32)
                        if not have_sums:
                            e.tensor_scalar(junk[:], Xt(t), 0.0, None, ALU.add, ALU.add,
                                            accum_out=sums[:, t:t + 1])
                        e.scalar_tensor_tensor(junk[:], Xt(t), 1.0, Xt(t), ALU.mult,
                                               ALU.mult, accum_out=sumsq[:, t:t + 1])
                e.drain()
                e.tensor_scalar(mcol[:], sums[:], 1.0 / D, None, ALU.mult)
                e.tensor_scalar(vcol[:], sumsq[:], 1.0 / D, None, ALU.mult)
                e.drain()
                e.scalar_tensor_tensor(tcol[:], mcol[:], 1.0, mcol[:], ALU.mult, ALU.mult)
                e.drain()
                e.tensor_sub(vcol[:], vcol[:], tcol[:])
                e.drain()
                e.tensor_scalar(vcol[:], vcol[:], eps_mul, eps_add, ALU.mult,
                                ALU.add)
                e.drain().then_inc(sV.h, 1)
            ph.on("vector", v1); sV.n += 1
            v1_at = sV.n
            def s1(e, v1_at=v1_at):
                e.wait_ge(sV.h, v1_at)
                e.activation(tcol[:], vcol[:], AF.Ln)
                e.drain()
                e.activation(scol[:], tcol[:], AF.Exp, scale=-0.5)
                e.drain().then_inc(sA.h, 1)
            ph.on("scalar", s1); sA.n += 1
            s1_at = sA.n
            def v2(e, s1_at=s1_at):
                e.wait_ge(sA.h, s1_at)
                e.tensor_scalar(bcol[:], mcol[:], -1.0, None, ALU.mult)
                e.drain()
                e.tensor_mul(bcol[:], bcol[:], scol[:])
                e.drain().then_inc(sV.h, 1)
            ph.on("vector", v2); sV.n += 1
            return sV.n

        def emit_lnT(ph, tiles, destT, dest_cols, stats_at, xtiles, nw=None):
            n = len(tiles)
            ap_at, ev_at, mm_at = {}, {}, {}

            def mk_apply(i):
                t, par = tiles[i], i % 2
                def ap(e, t=t, par=par):
                    e.wait_ge(sV.h, stats_at)
                    e.activation(xtiles[par][:], Xt(t), AF.Identity,
                                 bias=bcol[:, t:t + 1],
                                 scale=scol[:, t:t + 1])
                    e.drain().then_inc(sA.h, 1)
                ph.on("scalar", ap); sA.n += 1
                ap_at[i] = sA.n

            def mk_tmm(i):
                par = i % 2
                def tmm(e, i=i, par=par):
                    e.wait_ge(sA.h, ap_at[i])
                    e.wait_ge(sC.h, IDENT_AT)
                    if i >= 2:
                        e.wait_ge(sV.h, ev_at[i - 2])
                    for dc in range(ND):
                        bank = par * 2 + dc // 4
                        off = bank * 512 + (dc % 4) * 128
                        ins = e.matmul(pB[:, off:off + 128],
                                       xtiles[par][:, dc * 128:(dc + 1) * 128],
                                       ident[:], start=True, stop=True)
                    ins.then_inc(sT.h, 1)
                ph.on("tensor", tmm); sT.n += 1
                mm_at[i] = sT.n

            def mk_evac(i):
                par = i % 2
                def ev(e, i=i, par=par):
                    e.wait_ge(sT.h, mm_at[i])
                    if nw is None:
                        for g in range(2):
                            src = pB[:, (par * 2 + g) * 512:(par * 2 + g + 1) * 512]
                            dv = destT[:].rearrange("p (c u) -> p c u", u=dest_cols)[
                                :, g * 4:(g + 1) * 4, i * 128:i * 128 + 128]
                            e.tensor_copy(dv, src)
                    else:
                        for dc in range(ND):
                            bank = par * 2 + dc // 4
                            off = bank * 512 + (dc % 4) * 128
                            dv = destT[:, dc * dest_cols + i * 128:
                                       dc * dest_cols + i * 128 + 128]
                            e.tensor_mul(dv, pB[:, off:off + 128],
                                         nw[:, dc:dc + 1].broadcast_to([128, 128]))
                    e.drain().then_inc(sV.h, 1)
                ph.on("vector", ev); sV.n += 1
                ev_at[i] = sV.n

            mk_apply(0)
            if n > 1:
                mk_apply(1)
            for i in range(n):
                mk_tmm(i)
                mk_evac(i)
                if i + 2 < n:
                    mk_apply(i + 2)
            return sA.n

        def emit_out_residual(phname, src_AO, rhs_fn, nK, brow_info, scale, tiles,
                              target, accum, wtensor_at, transpose_src=True,
                              lhsT_fn=None, ln_stats=None):
            ph = Phase(nc, phname)
            ev_hist, tr_ev, prj_at = {}, {}, {}
            st_at = {}
            n = len(tiles)

            def mk_tr(i):
                t, par = tiles[i], i % 2
                def tmm(e, i=i, t=t, par=par):
                    e.wait_ge(sC.h, IDENT_AT)
                    if i >= 2:
                        e.wait_ge(sA.h, tr_ev[i - 2])
                    for dc in range(ND):
                        bank = par * 2 + dc // 4
                        off = bank * 512 + (dc % 4) * 128
                        ins = e.matmul(pB[:, off:off + 128],
                                       src_AO[:, t * 1024 + dc * 128:t * 1024 + dc * 128 + 128],
                                       ident[:], start=True, stop=True)
                    ins.then_inc(sT.h, 1)
                ph.on("tensor", tmm); sT.n += 1
                mm = sT.n
                def ev(e, i=i, par=par, mm=mm):
                    e.wait_ge(sT.h, mm)
                    if i >= 2:
                        e.wait_ge(sT.h, prj_at[i - 2])
                    for g in range(2):
                        e.activation(aot[par][:, g * 512:(g + 1) * 512],
                                     pB[:, (par * 2 + g) * 512:(par * 2 + g + 1) * 512],
                                     AF.Identity)
                    e.drain().then_inc(sA.h, 1)
                ph.on("scalar", ev); sA.n += 1
                tr_ev[i] = sA.n

            def mk_prj(i):
                t, par = tiles[i], i % 2
                psum = pA[:, par * 1024:(par + 1) * 1024]
                def mm(e, i=i, t=t, par=par, psum=psum):
                    e.wait_ge(wtensor_at[0].h, wtensor_at[1])
                    if transpose_src:
                        e.wait_ge(sA.h, tr_ev[i])
                    if i >= 2:
                        e.wait_ge(sV.h, ev_hist[i - 2])
                    for nn in range(2):
                        for kc in range(nK):
                            lh = (aot[par][:, kc * 128:(kc + 1) * 128]
                                  if transpose_src else lhsT_fn(t, kc))
                            ins = e.matmul(psum[:, nn * 512:(nn + 1) * 512], lh,
                                           rhs_fn(kc, nn), start=(kc == 0),
                                           stop=(kc == nK - 1 and brow_info is None))
                        if brow_info is not None:
                            bw, bsem, bat = brow_info
                            e.wait_ge(bsem.h, bat)
                            ins = e.matmul(psum[:, nn * 512:(nn + 1) * 512],
                                           onesr[:], bw[:, nn * 512:(nn + 1) * 512],
                                           start=False, stop=True)
                    ins.then_inc(sT.h, 1)
                ph.on("tensor", mm); sT.n += 1
                prj_at[i] = sT.n
                def ev(e, i=i, t=t, par=par, psum=psum, mmv=sT.n):
                    e.wait_ge(sT.h, mmv)
                    if target == "out" and i >= 2:
                        e.wait_ge(sO.h, st_at[i - 2])
                    dst = Xt(t) if target == "X" else outst_buf[
                        :, (i % 2) * 1024:(i % 2 + 1) * 1024]
                    kw = {"accum_out": sums[:, t:t + 1]} if accum else {}
                    e.scalar_tensor_tensor(dst, psum[:], scale, Xt(t),
                                           ALU.mult, ALU.add, **kw)
                    if ln_stats is not None:
                        e.drain()
                        e.scalar_tensor_tensor(junk[:], Xt(t), 1.0, Xt(t), ALU.mult,
                                               ALU.mult, accum_out=sumsq[:, t:t + 1])
                    e.drain().then_inc(sV.h, 1)
                ph.on("vector", ev); sV.n += 1
                ev_hist[i] = sV.n
                if target == "out":
                    def st(e, t=t, i=i, v_at=sV.n):
                        e.wait_ge(sV.h, v_at)
                        e.dma_start(y[t * 128:(t + 1) * 128, :],
                                    outst_buf[:, (i % 2) * 1024:(i % 2 + 1) * 1024]
                                    ).then_inc(sO.h, 16)
                    ph.on("sync", st); sO.n += 16
                    st_at[i] = sO.n

            if transpose_src:
                mk_tr(0)
                for i in range(n):
                    if i + 1 < n:
                        mk_tr(i + 1)
                    mk_prj(i)
            else:
                for i in range(n):
                    mk_prj(i)
            st_ret = None
            if ln_stats is not None and ln_stats != "inline":
                st_ret = emit_ln_stats(ph, have_sums=True, rms_fold=ln_stats,
                                       have_sumsq=True)
            ph.emit()
            return st_ret

        def emit_attn_a(only_ln=False):
            gL = contextlib.ExitStack()
            gR = contextlib.ExitStack()
            xT = gR.enter_context(nc.sbuf_tensor("a_xT", [128, ND * 2048], BF16, side="right"))
            xtiles = [gR.enter_context(nc.sbuf_tensor(f"a_xt{i}", [128, 1024], BF16, side="right"))
                      for i in range(2)]
            qaggT = gL.enter_context(nc.sbuf_tensor("qaggT", [128, ND * 2048], BF16))
            kaggT = gL.enter_context(nc.sbuf_tensor("kaggT", [128, ND * 2048], BF16))
            vcat = gL.enter_context(nc.sbuf_tensor("a_vcat", [128, NT * 1040], BF16))

            gV = contextlib.ExitStack()
            bvr = gV.enter_context(nc.sbuf_tensor("bvr", [1, D], BF16, side="right"))
            wres_v = gV.enter_context(nc.sbuf_tensor("a_wrv", [128, ND * 1024], BF16, side="right"))
            ph = Phase(nc, "a_ln1")
            st_at = emit_ln_stats(ph, have_sums=False, rms_fold=False)
            emit_lnT(ph, list(range(NT)), xT, 2048, st_at, xtiles)
            def vpre(e):
                e.dma_start(bvr[:], dr["a_vb"]).then_inc(sG.h, 16)
                for b in range(2):
                    e.dma_start(
                        wres_v[:, b * 4096:(b + 1) * 4096].rearrange(
                            "p (c m) -> p c m", c=4),
                        dr["a_vw"][b * 4:(b + 1) * 4].transpose([1, 0, 2])
                    ).then_inc(sG.h, 16)
            ph.on("gpsimd", vpre); sG.n += 48
            vload_at = sG.n
            ph.emit()
            if only_ln:
                emit_dump("d_xT", xT)
                gR.close()
                gL.close()
                return

            cbuf = [gV.enter_context(nc.sbuf_tensor(f"vcb{i}", [128, 1024], F32, side="right"))
                    for i in range(2)]
            acc = gV.enter_context(nc.sbuf_tensor("vacc", [128, 1024], F32, side="right"))
            ph = Phase(nc, "a_v")
            def vc_ones(e):
                e.memset(vcat[:].rearrange("p (t h c) -> p t h c", t=NT, c=65)[
                    :, :, :, 64:65], 1.0)
                e.drain().then_inc(sV.h, 1)
            ph.on("vector", vc_ones); sV.n += 1
            ev2, lif2 = {}, {}
            for t in range(NT):
                par = t % 2
                psum = pA[:, par * 1024:(par + 1) * 1024]
                def mm(e, t=t, psum=psum):
                    e.wait_ge(sG.h, vload_at)
                    if t >= 2:
                        e.wait_ge(sA.h, ev2[t - 2])
                    for n in range(2):
                        for kc in range(ND):
                            e.matmul(psum[:, n * 512:(n + 1) * 512],
                                     xT[:, kc * 2048 + t * 128:kc * 2048 + t * 128 + 128],
                                     wres_v[:, kc * 1024 + n * 512:kc * 1024 + n * 512 + 512],
                                     start=(kc == 0), stop=False)
                        ins = e.matmul(psum[:, n * 512:(n + 1) * 512], onesr[:],
                                       bvr[:, n * 512:(n + 1) * 512],
                                       start=False, stop=True)
                    ins.then_inc(sT.h, 1)
                ph.on("tensor", mm); sT.n += 1
                mm_at = sT.n
                def ev(e, par=par, psum=psum, mm_at=mm_at, t=t):
                    e.wait_ge(sT.h, mm_at)
                    if t >= 2:
                        e.wait_ge(sV.h, lif2[t - 2])
                    e.activation(cbuf[par][:], psum, AF.Identity)
                    e.drain().then_inc(sA.h, 1)
                ph.on("scalar", ev); sA.n += 1
                ev2[t] = sA.n
                def lif(e, t=t, par=par, a_at=sA.n):
                    e.wait_ge(sA.h, a_at)
                    cb3 = cbuf[par][:].rearrange("p (h c) -> p h c", c=64)
                    a3 = acc[:].rearrange("p (h c) -> p h c", c=64)
                    cb = cbuf[par][:]
                    a = acc[:]
                    e.tensor_scalar(a, cb, TH[0], None, ALU.is_gt)
                    e.scalar_tensor_tensor(a, cb, TH[1], a, ALU.is_gt, ALU.add)
                    e.scalar_tensor_tensor(a, cb, TH[2], a, ALU.is_gt, ALU.add)
                    dstv = vcat[:].rearrange("p (u h c) -> p u h c", u=NT, c=65)[
                        :, t, :, 0:64]
                    e.scalar_tensor_tensor(dstv, cb3, TH[3], a3, ALU.is_gt,
                                           ALU.add)
                    e.drain().then_inc(sV.h, 1)
                ph.on("vector", lif); sV.n += 1
                lif2[t] = sV.n
            ph.emit()
            gV.close()

            gQ = contextlib.ExitStack()
            wring = [gQ.enter_context(nc.sbuf_tensor(f"awr{i}", [128, 2048], BF16, side="right"))
                     for i in range(3)]
            bq = gQ.enter_context(nc.sbuf_tensor("bq", [128, ND], F32, side="right"))
            bk = gQ.enter_context(nc.sbuf_tensor("bk", [128, ND], F32, side="right"))
            cbuf = [gQ.enter_context(nc.sbuf_tensor(f"qcb{i}", [128, 1024], F32, side="right"))
                    for i in range(2)]
            acc = gQ.enter_context(nc.sbuf_tensor("qacc", [128, 1024], F32, side="right"))
            ph = Phase(nc, "a_qk")
            ph.on("sync", lambda e: e.dma_start(bq[:], dr["a_qb"]).then_inc(sC.h, 16))
            sC.n += 16
            ph.on("sync", lambda e: e.dma_start(bk[:], dr["a_kb"]).then_inc(sC.h, 16))
            sC.n += 16
            bias_at = sC.n
            slabs = [("a_qw", dc) for dc in range(ND)] + [("a_kw", dc) for dc in range(ND)]
            pairs = [("a_qw", 2 * p) for p in range(4)] + [("a_kw", 2 * p) for p in range(4)]
            pair_at = [None] * len(pairs)
            def ld_pair(e, jp, slot):
                wn, d0 = pairs[jp]
                e.dma_start(wring[slot][:].rearrange("p (c m) -> p c m", c=2),
                            dr[wn][d0:d0 + 2].transpose([1, 0, 2])).then_inc(sG.h, 16)
            for jp in range(3):
                ph.on("gpsimd", lambda e, jp=jp: ld_pair(e, jp, jp % 3))
                sG.n += 16
                pair_at[jp] = sG.n
            ev_hist, lif_hist = {}, {}
            iters = [(j, n2) for j in range(len(slabs)) for n2 in range(2)]
            for it, (j, n2) in enumerate(iters):
                wn, dc = slabs[j]
                jp, par = j // 2, it % 4
                slot = jp % 3
                scol0 = (j % 2) * 1024
                psum = (pA if par < 2 else pB)[:, (par % 2) * 1024:(par % 2 + 1) * 1024]
                dst = qaggT if wn == "a_qw" else kaggT
                bias = bq if wn == "a_qw" else bk
                def mm(e, it=it, j=j, n2=n2, slot=slot, scol0=scol0, psum=psum,
                       g_at=pair_at[jp]):
                    e.wait_ge(sG.h, g_at)
                    if it >= 2:
                        e.wait_ge(sA.h, ev_hist[it - 2])
                    for n in range(2):
                        for kc in range(ND):
                            col = n2 * 1024 + n * 512
                            ins = e.matmul(
                                psum[:, n * 512:(n + 1) * 512],
                                wring[slot][:, scol0 + kc * 128:scol0 + (kc + 1) * 128],
                                xT[:, kc * 2048 + col:kc * 2048 + col + 512],
                                start=(kc == 0), stop=(kc == ND - 1))
                    ins.then_inc(sT.h, 1)
                ph.on("tensor", mm); sT.n += 1
                mm_at = sT.n
                if n2 == 1 and j % 2 == 1 and jp + 3 < len(pairs):
                    def pre(e, jp=jp, slot=slot, mm_at=mm_at):
                        e.wait_ge(sT.h, mm_at)
                        ld_pair(e, jp + 3, slot)
                    ph.on("gpsimd", pre); sG.n += 16
                    pair_at[jp + 3] = sG.n
                def ev(e, it=it, psum=psum, bias=bias, dc=dc, mm_at=mm_at):
                    e.wait_ge(sT.h, mm_at)
                    e.wait_ge(sC.h, bias_at)
                    if it >= 2:
                        e.wait_ge(sV.h, lif_hist[it - 2])
                    e.activation(cbuf[it % 2][:], psum[:], AF.Identity,
                                 bias=bias[:, dc:dc + 1])
                    e.drain().then_inc(sA.h, 1)
                ph.on("scalar", ev); sA.n += 1
                ev_hist[it] = sA.n
                def lif(e, it=it, n2=n2, dst=dst, dc=dc, a_at=sA.n):
                    e.wait_ge(sA.h, a_at)
                    cb = cbuf[it % 2]
                    e.tensor_scalar(acc[:], cb[:], TH[0], None, ALU.is_gt)
                    e.scalar_tensor_tensor(acc[:], cb[:], TH[1], acc[:], ALU.is_gt, ALU.add)
                    e.scalar_tensor_tensor(acc[:], cb[:], TH[2], acc[:], ALU.is_gt, ALU.add)
                    e.scalar_tensor_tensor(
                        dst[:, dc * 2048 + n2 * 1024:dc * 2048 + n2 * 1024 + 1024],
                        cb[:], TH[3], acc[:], ALU.is_gt, ALU.add)
                    e.drain().then_inc(sV.h, 1)
                ph.on("vector", lif); sV.n += 1
                lif_hist[it] = sV.n
            ph.emit()
            emit_dump("d_xT", xT)
            emit_dump("d_q", qaggT)
            emit_dump("d_k", kaggT)
            emit_dump("d_v", vcat)
            gQ.close()
            gR.close()

            g4 = contextlib.ExitStack()
            AO = g4.enter_context(nc.sbuf_tensor("a_AO", [128, NT * 1024], BF16))
            bor = g4.enter_context(nc.sbuf_tensor("bor", [1, D], BF16))
            gE = contextlib.ExitStack()
            ETb = [gE.enter_context(nc.sbuf_tensor(f"a_ET{i}", [128, 16 * 512], BF16))
                   for i in range(2)]

            ph = Phase(nc, "a_sc")
            ph.on("gpsimd", lambda e: e.dma_start(bor[:], dr["a_ob"]).then_inc(sG.h, 16))
            sG.n += 16
            bor_at = sG.n

            exp_hist, av_done, ao_ev = {}, {}, {}
            pend_sc, pend_av = [], []
            GRP = [(0, 4), (4, 3), (7, 4), (11, 3), (14, 2)]
            GP = [0, 1, 0, 1, 0]
            units = [(h, qb) for h in range(H) for qb in range(4)]
            for u, (h, qb) in enumerate(units):
                hc, off = h // 2, (h % 2) * 64
                for g, (kc0, nkc) in enumerate(GRP):
                    pP = pA[:, 0:2048] if GP[g] == 0 else pB[:, 0:1536]
                    def smm(e, u=u, g=g, kc0=kc0, nkc=nkc, hc=hc, off=off, qb=qb, pP=pP):
                        key = (u, g - 2) if g >= 2 else (u - 1, 4 if g == 0 else 3)
                        if key in exp_hist:
                            e.wait_ge(sA.h, exp_hist[key])
                        for k in range(nkc):
                            kc = kc0 + k
                            ins = e.matmul(
                                pP[:, k * 512:(k + 1) * 512],
                                kaggT[off:off + 64,
                                      hc * 2048 + kc * 128:hc * 2048 + kc * 128 + 128],
                                qaggT[off:off + 64,
                                      hc * 2048 + qb * 512:hc * 2048 + qb * 512 + 512],
                                start=True, stop=True)
                        ins.then_inc(sT.h, 1)
                    ph.on("tensor", smm); sT.n += 1
                    s_at = sT.n
                    def ex(e, u=u, g=g, kc0=kc0, nkc=nkc, pP=pP, s_at=s_at):
                        e.wait_ge(sT.h, s_at)
                        if u >= 2 and (u - 2) in av_done:
                            e.wait_ge(sT.h, av_done[u - 2])
                        ins = e.activation(ETb[u % 2][:, kc0 * 512:(kc0 + nkc) * 512],
                                           pP[:, 0:nkc * 512],
                                           AF.Exp, bias=neg4[:], scale=1.0 / 128.0)
                        if g == 4:
                            e.drain()
                        ins.then_inc(sA.h, 1)
                    ph.on("scalar", ex); sA.n += 1
                    exp_hist[(u, g)] = sA.n
                exp_all = sA.n
                if pend_av:
                    ua, af = pend_av.pop(0)
                    ph.on("tensor", af); sT.n += 1
                    av_done[ua] = sT.n
                if pend_sc:
                    up, scf = pend_sc.pop(0)
                    ph.on("vector", scf); sV.n += 1
                    ao_ev[up] = sV.n
                def avmm(e, u=u, h=h, exp_all=exp_all):
                    e.wait_ge(sA.h, exp_all)
                    if u >= 1:
                        e.wait_ge(sV.h, ao_ev[u - 1])
                    for kc in range(16):
                        for mc in range(4):
                            ins = e.matmul(
                                pB[:, 1536 + mc * 128:1536 + mc * 128 + 65],
                                ETb[u % 2][:, kc * 512 + mc * 128:kc * 512 + mc * 128 + 128],
                                vcat[:, kc * 1040 + h * 65:kc * 1040 + h * 65 + 65],
                                start=(kc == 0 and mc == 0),
                                stop=(kc == 15 and mc == 3),
                                skip_group_check=True)
                    ins.then_inc(sT.h, 1)
                pend_av.append((u, avmm))
                def recev(e, u=u, h=h, qb=qb):
                    e.wait_ge(sT.h, av_done[u])
                    rc = (u % 2) * 4
                    for mc in range(4):
                        e.reciprocal(rcol[:, rc + mc:rc + mc + 1],
                                     pB[:, 1536 + mc * 128 + 64:1536 + mc * 128 + 65])
                    e.drain()
                    e.tensor_scalar(rcol[:, rc:rc + 4], rcol[:, rc:rc + 4], 0.25,
                                    None, ALU.mult)
                    e.drain()
                    for mc in range(4):
                        t = qb * 4 + mc
                        e.tensor_mul(
                            AO[:, t * 1024 + h * 64:t * 1024 + h * 64 + 64],
                            pB[:, 1536 + mc * 128:1536 + mc * 128 + 64],
                            rcol[:, rc + mc:rc + mc + 1].broadcast_to([128, 64]))
                    e.drain().then_inc(sV.h, 1)
                pend_sc.append((u, recev))
            while pend_av:
                ua, af = pend_av.pop(0)
                ph.on("tensor", af); sT.n += 1
                av_done[ua] = sT.n
            while pend_sc:
                up, scf = pend_sc.pop(0)
                ph.on("vector", scf); sV.n += 1
                ao_ev[up] = sV.n
            ph.emit()
            gE.close()
            emit_dump("d_ao", AO)

            g5 = contextlib.ExitStack()
            wres = g5.enter_context(nc.sbuf_tensor("a_wro", [128, ND * 1024], BF16))
            ph = Phase(nc, "a_wo")
            def oslabs(e):
                for b in range(2):
                    e.dma_start(
                        wres[:, b * 4096:(b + 1) * 4096].rearrange(
                            "p (c m) -> p c m", c=4),
                        dr["a_ow"][b * 4:(b + 1) * 4].transpose([1, 0, 2])
                    ).then_inc(sG.h, 16)
            ph.on("gpsimd", oslabs); sG.n += 32
            oload_at = sG.n
            ph.emit()
            st_ret = emit_out_residual(
                "a_out", AO,
                rhs_fn=lambda kc, n: wres[:, kc * 1024 + n * 512:kc * 1024 + n * 512 + 512],
                nK=ND, brow_info=(bor, sG, bor_at), scale=1.0,
                tiles=list(range(NT)), target="X", accum=True,
                wtensor_at=(sG, oload_at), ln_stats=False)
            g5.close()
            g4.close()
            gL.close()
            return st_ret

        def emit_ffn(app, last, stats_in):
            gL = contextlib.ExitStack()
            gR = contextlib.ExitStack()
            hb = gL.enter_context(nc.sbuf_tensor(f"f{app}_hb", [128, NHC * 1024], BF16))
            wdT = gL.enter_context(nc.sbuf_tensor(f"f{app}_wdT", [128, NHC * 1024], BF16))
            global outst_buf
            if last:
                outst_buf = gL.enter_context(nc.sbuf_tensor(f"outst{app}", [128, 2048], F32))
            xTf = gR.enter_context(nc.sbuf_tensor(f"f{app}_xT", [128, ND * 1024], BF16, side="right"))
            xtiles = [gR.enter_context(nc.sbuf_tensor(f"f{app}_xt{i}", [128, 1024], BF16, side="right"))
                      for i in range(2)]
            wgr = [gR.enter_context(nc.sbuf_tensor(f"f{app}_wg{i}", [128, 2048], BF16, side="right"))
                   for i in range(2)]
            wur = [gR.enter_context(nc.sbuf_tensor(f"f{app}_wu{i}", [128, 2048], BF16, side="right"))
                   for i in range(2)]
            sg = [gR.enter_context(nc.sbuf_tensor(f"f{app}_sg{i}", [128, 1024], BF16, side="right"))
                  for i in range(2)]
            ub = [gR.enter_context(nc.sbuf_tensor(f"f{app}_ub{i}", [128, 1024], BF16, side="right"))
                  for i in range(2)]

            st_at = stats_in
            wd_at = None
            for half in range(2):
                tiles = list(range(half * 8, half * 8 + 8))
                ph = Phase(nc, f"f{app}h{half}_tr")
                emit_lnT(ph, tiles, xTf, 1024, st_at, xtiles)
                gu_at = [None] * NHC
                def ld_gu(e, p, slot):
                    e.dma_start(wgr[slot][:].rearrange("p (c m) -> p c m", c=2),
                                dr["f_wg"][2 * p:2 * p + 2].transpose([1, 0, 2])
                                ).then_inc(sG.h, 16)
                    e.dma_start(wur[slot][:].rearrange("p (c m) -> p c m", c=2),
                                dr["f_wu"][2 * p:2 * p + 2].transpose([1, 0, 2])
                                ).then_inc(sG.h, 16)
                for p in range(2):
                    ph.on("gpsimd", lambda e, p=p: ld_gu(e, p, p % 2))
                    sG.n += 32
                    gu_at[2 * p] = sG.n
                    gu_at[2 * p + 1] = sG.n
                if half == 0:
                    def wdl(e):
                        for b in range(4):
                            e.dma_start(
                                wdT[:, b * 6144:(b + 1) * 6144].rearrange(
                                    "p (c m) -> p c m", c=6),
                                dr["f_wd"][b * 6:(b + 1) * 6].transpose([1, 0, 2])
                            ).then_inc(sG.h, 16)
                    ph.on("gpsimd", wdl); sG.n += 64
                    wd_at = sG.n
                ph.emit()

                ph = Phase(nc, f"f{app}h{half}_gu")
                ev_h, hp_h = {}, {}
                for hc in range(NHC):
                    par = hc % 2
                    slot = (hc // 2) % 2
                    scol0 = (hc % 2) * 1024
                    pg = pA[:, par * 1024:(par + 1) * 1024]
                    pu = pB[:, par * 1024:(par + 1) * 1024]
                    def mm(e, hc=hc, par=par, slot=slot, scol0=scol0, pg=pg, pu=pu,
                           g_at=gu_at[hc]):
                        e.wait_ge(sG.h, g_at)
                        if hc >= 2:
                            e.wait_ge(sA.h, ev_h[hc - 2])
                        for n in range(2):
                            for kc in range(ND):
                                e.matmul(pg[:, n * 512:(n + 1) * 512],
                                         wgr[slot][:, scol0 + kc * 128:scol0 + (kc + 1) * 128],
                                         xTf[:, kc * 1024 + n * 512:kc * 1024 + n * 512 + 512],
                                         start=(kc == 0), stop=(kc == ND - 1))
                            for kc in range(ND):
                                ins = e.matmul(pu[:, n * 512:(n + 1) * 512],
                                               wur[slot][:, scol0 + kc * 128:scol0 + (kc + 1) * 128],
                                               xTf[:, kc * 1024 + n * 512:kc * 1024 + n * 512 + 512],
                                               start=(kc == 0), stop=(kc == ND - 1))
                        ins.then_inc(sT.h, 1)
                    ph.on("tensor", mm); sT.n += 1
                    mm_at = sT.n
                    if hc % 2 == 1 and hc + 4 < NHC:
                        def pre(e, hc=hc, slot=slot, mm_at=mm_at):
                            e.wait_ge(sT.h, mm_at)
                            ld_gu(e, (hc + 4) // 2, slot)
                        ph.on("gpsimd", pre); sG.n += 32
                        gu_at[hc + 3] = sG.n
                        gu_at[hc + 4] = sG.n
                    def ev(e, hc=hc, par=par, pg=pg, pu=pu, mm_at=mm_at):
                        e.wait_ge(sT.h, mm_at)
                        if hc >= 2:
                            e.wait_ge(sV.h, hp_h[hc - 2])
                        e.activation(sg[par][:], pg, AF.Silu, scale=DQ1)
                        e.activation(ub[par][:], pu, AF.Identity)
                        e.drain().then_inc(sA.h, 1)
                    ph.on("scalar", ev); sA.n += 1
                    ev_h[hc] = sA.n
                    def hm(e, hc=hc, par=par, a_at=sA.n):
                        e.wait_ge(sA.h, a_at)
                        e.tensor_mul(hb[:, hc * 1024:(hc + 1) * 1024], sg[par][:],
                                     ub[par][:])
                        e.drain().then_inc(sV.h, 1)
                    ph.on("vector", hm); sV.n += 1
                    hp_h[hc] = sV.n
                ph.emit()

                lnspec = None
                if not last:
                    lnspec = "inline" if half == 0 else (app == 1)
                st_next = emit_out_residual(
                    f"f{app}h{half}_dn", None,
                    rhs_fn=lambda kc, n: wdT[:, kc * 1024 + n * 512:kc * 1024 + n * 512 + 512],
                    nK=NHC, brow_info=None, scale=DQ2, tiles=tiles,
                    target=("out" if last else "X"), accum=not last,
                    wtensor_at=(sG, wd_at), transpose_src=False,
                    lhsT_fn=lambda t, kc: hb[:, kc * 1024 + (t % 8) * 128:
                                             kc * 1024 + (t % 8) * 128 + 128],
                    ln_stats=lnspec)
            gR.close()
            gL.close()
            return st_next

        def emit_attn_b(stats_in, stop=99):
            gL = contextlib.ExitStack()
            Q = gL.enter_context(nc.sbuf_tensor("b_Q", [128, NT * 1024], BF16))
            K = gL.enter_context(nc.sbuf_tensor("b_K", [128, NT * 1024], BF16))
            vcat = gL.enter_context(nc.sbuf_tensor("b_vcat", [128, NT * 1040], BF16))

            gR = contextlib.ExitStack()
            xT = gR.enter_context(nc.sbuf_tensor("b_xT", [128, ND * 2048], BF16, side="right"))
            xtiles = [gR.enter_context(nc.sbuf_tensor(f"b_xt{i}", [128, 1024], BF16, side="right"))
                      for i in range(2)]
            nw = gR.enter_context(nc.sbuf_tensor("b_nwc", [128, ND], F32, side="right"))
            wq = [gR.enter_context(nc.sbuf_tensor(f"b_wq{i}", [128, ND * 512], BF16, side="right"))
                  for i in range(2)]
            ph = Phase(nc, "b_ln")
            ph.on("sync", lambda e: e.dma_start(nw[:], dr["b_nw"]).then_inc(sC.h, 16))
            sC.n += 16
            nw_at = sC.n
            st_at = stats_in
            ph.on("scalar", lambda e: e.wait_ge(sC.h, nw_at))
            emit_lnT(ph, list(range(NT)), xT, 2048, st_at, xtiles, nw=nw)
            wq_at = [None] * 6
            for nn in range(2):
                def pre0(e, nn=nn):
                    e.dma_start(wq[nn][:].rearrange("p (dc c) -> p dc c", dc=ND),
                                dr["b_wqkv"][nn].transpose([1, 0, 2])
                                ).then_inc(sG.h, 16)
                ph.on("gpsimd", pre0); sG.n += 16
                wq_at[nn] = sG.n
            ph.emit()
            if stop <= 1:
                emit_dump("d_xT", xT)
                gR.close(); gL.close()
                return

            cs = [gR.enter_context(nc.sbuf_tensor(f"b_cs{i}", [128, 2048], BF16, side="right"))
                  for i in range(2)]
            tmp = gR.enter_context(nc.sbuf_tensor("b_tmp", [128, 1024], BF16, side="right"))
            ph = Phase(nc, "b_qkv")
            cs_at = [None] * 32
            for j in range(2):
                ph.on("sync", lambda e, j=j: e.dma_start(
                    cs[j][:], dr["c_cossin"][(j % NT) * 128:(j % NT + 1) * 128, :]
                    ).then_inc(sC.h, 16))
                sC.n += 16
                cs_at[j] = sC.n
            rope_ct = [0]
            rope_sv = {}
            def emit_rope_job(jb):
                buf = Q if jb < 16 else K
                t = jb % 16
                dep = ev_q[(1, t)] if jb < 16 else ev_q[(3, t)]
                def rp(e, jb=jb, buf=buf, t=t, dep=dep, c_at=cs_at[jb]):
                    e.wait_ge(sC.h, c_at)
                    e.wait_ge(sA.h, dep)
                    bt = buf[:, t * 1024:(t + 1) * 1024]
                    swl = bt.rearrange("p (h two c) -> p h two c", h=H, two=2)
                    tw = tmp[:].rearrange("p (h two c) -> p h two c", h=H, two=2)
                    e.tensor_copy(tw[:, :, 0, :], swl[:, :, 1, :])
                    e.tensor_copy(tw[:, :, 1, :], swl[:, :, 0, :])
                    e.drain()
                    e.tensor_mul(bt, bt, cs[jb % 2][:, 0:1024])
                    e.tensor_mul(tmp[:], tmp[:], cs[jb % 2][:, 1024:2048])
                    e.drain()
                    e.tensor_add(bt, bt, tmp[:])
                    e.drain().then_inc(sV.h, 1)
                ph.on("vector", rp); sV.n += 1
                rope_sv[jb] = sV.n
                if jb + 2 < 32:
                    def pre(e, jb=jb, v_at=sV.n):
                        e.wait_ge(sV.h, v_at)
                        e.dma_start(cs[jb % 2][:],
                                    dr["c_cossin"][((jb + 2) % NT) * 128:
                                                   ((jb + 2) % NT + 1) * 128, :]
                                    ).then_inc(sC.h, 16)
                    ph.on("sync", pre); sC.n += 16
                    cs_at[jb + 2] = sC.n
            def vc_ones(e):
                e.memset(vcat[:].rearrange("p (t h c) -> p t h c", t=NT, c=65)[
                    :, :, :, 64:65], 1.0)
                e.drain().then_inc(sV.h, 1)
            ph.on("vector", vc_ones); sV.n += 1
            ev_q = {}
            for nn in range(6):
                par = nn % 2
                for t in range(NT):
                    bpar = t % 4
                    psum = pA[:, bpar * 512:(bpar + 1) * 512]
                    def mm(e, nn=nn, t=t, par=par, psum=psum, g_at=wq_at[nn]):
                        e.wait_ge(sG.h, g_at)
                        key = (nn, t - 4) if t >= 4 else (nn - 1, NT + t - 4)
                        if key in ev_q:
                            e.wait_ge(sA.h, ev_q[key])
                        for kc in range(ND):
                            ins = e.matmul(psum,
                                           xT[:, kc * 2048 + t * 128:kc * 2048 + t * 128 + 128],
                                           wq[par][:, kc * 512:(kc + 1) * 512],
                                           start=(kc == 0), stop=(kc == ND - 1))
                        ins.then_inc(sT.h, 1)
                    ph.on("tensor", mm); sT.n += 1
                    mm_at = sT.n
                    def ev(e, nn=nn, t=t, psum=psum, mm_at=mm_at):
                        e.wait_ge(sT.h, mm_at)
                        if nn < 2:
                            dv = Q[:, t * 1024 + nn * 512:t * 1024 + nn * 512 + 512]
                            sv = psum
                        elif nn < 4:
                            dv = K[:, t * 1024 + (nn - 2) * 512:t * 1024 + (nn - 2) * 512 + 512]
                            sv = psum
                        else:
                            h0 = 8 * (nn - 4)
                            dv = vcat[:].rearrange("p (u h c) -> p u h c", u=NT, c=65)[
                                :, t, h0:h0 + 8, 0:64]
                            sv = psum.rearrange("p (h c) -> p h c", c=64)
                        e.activation(dv, sv, AF.Identity)
                        e.drain().then_inc(sA.h, 1)
                    ph.on("scalar", ev); sA.n += 1
                    ev_q[(nn, t)] = sA.n
                    if nn == 2:
                        emit_rope_job(t)
                    elif nn == 4:
                        emit_rope_job(16 + t)
                if nn + 2 < 6:
                    mm_prev = sT.n
                    def pre(e, nn=nn, par=par, mm_prev=mm_prev):
                        e.wait_ge(sT.h, mm_prev)
                        e.dma_start(wq[par][:].rearrange("p (dc c) -> p dc c", dc=ND),
                                    dr["b_wqkv"][nn + 2].transpose([1, 0, 2])
                                    ).then_inc(sG.h, 16)
                    ph.on("gpsimd", pre); sG.n += 16
                    wq_at[nn + 2] = sG.n
            ph.emit()
            if stop <= 2:
                emit_dump("d_q", Q)
                emit_dump("d_k", K)
                emit_dump("d_v", vcat)
                gR.close(); gL.close()
                return
            gR.close()

            rope_done = max(rope_sv.values())
            if stop <= 3:
                emit_dump("d_q", Q)
                emit_dump("d_k", K)
                gR.close(); gL.close()
                return
            gR.close()

            gR3 = contextlib.ExitStack()
            ra = lambda n, sh, dt: gR3.enter_context(nc.sbuf_tensor(n, sh, dt, side="right"))
            qTp = ra("b_qTp", [128, 2048], BF16)
            kTp = ra("b_kTp", [128, 2048], BF16)
            ETb = [ra(f"b_ET{i}", [128, 1024], BF16) for i in range(2)]
            ETpmb = [ra(f"b_ETpm{i}", [16, 256], BF16) for i in range(2)]
            mask = ra("b_mask", [128, NSEG * 512], BF16)
            pmk = ra("b_pmk_sb", [128, H * PM], BF16)
            pmv = ra("b_pmv_sb", [16, H * 65], BF16)
            gAO = contextlib.ExitStack()
            AO = gAO.enter_context(nc.sbuf_tensor("b_AO", [128, NT * 1024], BF16))

            ph = Phase(nc, "b_sc")
            ph.on("sync", lambda e: e.dma_start(mask[:], dr["c_mask"]).then_inc(sC.h, 16))
            sC.n += 16
            mask_at = sC.n
            ph.on("gpsimd", lambda e: e.dma_start(pmk[:], dr["b_pmkT"]).then_inc(sG.h, 16))
            sG.n += 16
            ph.on("gpsimd", lambda e: e.dma_start(pmv[:], dr["b_pmv"]).then_inc(sG.h, 16))
            sG.n += 16
            pm_at = sG.n

            tr_ev, exp_done, av_b, ao_b, expm_hist = {}, {}, {}, {}, {}
            pend_rec, pend_av = [], []
            uu = -1
            for hp in range(8):
                for gi in range(4):
                    buf = Q if gi < 2 else K
                    dstb = qTp if gi < 2 else kTp
                    t0 = (gi % 2) * 8
                    u_guard = exp_done.get(hp * 16 - 2 + gi % 2)
                    def trg(e, hp=hp, gi=gi, buf=buf, t0=t0, u_guard=u_guard):
                        e.wait_ge(sC.h, IDENT_AT)
                        e.wait_ge(sV.h, rope_done)
                        key = (hp, gi - 2) if gi >= 2 else (hp - 1, gi + 2)
                        if key in tr_ev:
                            e.wait_ge(sV.h, tr_ev[key])
                        if u_guard is not None:
                            e.wait_ge(sA.h, u_guard)
                        for tl in range(8):
                            bank = (gi % 2) * 2 + tl % 2
                            off = bank * 512 + (tl // 2) * 128
                            ins = e.matmul(
                                pA[:, off:off + 128],
                                buf[:, (t0 + tl) * 1024 + hp * 128:
                                    (t0 + tl) * 1024 + hp * 128 + 128],
                                ident[:], start=True, stop=True)
                        ins.then_inc(sT.h, 1)
                    ph.on("tensor", trg); sT.n += 1
                    tg_at = sT.n
                    def trev(e, gi=gi, dstb=dstb, t0=t0, tg_at=tg_at):
                        e.wait_ge(sT.h, tg_at)
                        for bk in range(2):
                            dv = dstb[:].rearrange("p (u c) -> p u c", c=128)[
                                :, t0 + bk:t0 + 8:2, :]
                            e.tensor_copy(dv, pA[:, ((gi % 2) * 2 + bk) * 512:
                                                  ((gi % 2) * 2 + bk + 1) * 512])
                        e.drain().then_inc(sV.h, 1)
                    ph.on("vector", trev); sV.n += 1
                    tr_ev[(hp, gi)] = sV.n
                tr_at = sV.n

                for hh in range(2):
                    h = hp * 2 + hh
                    off = hh * 64
                    for s in range(NSEG):
                        for qh in range(2):
                            uu += 1
                            par = uu % 2
                            pE = pA[:, par * 1024:(par + 1) * 1024]
                            def smm(e, s=s, qh=qh, off=off, pE=pE, uu=uu, tr_at=tr_at):
                                e.wait_ge(sV.h, tr_at)
                                if uu >= 2:
                                    e.wait_ge(sA.h, exp_done[uu - 2])
                                q0 = s * 512 + qh * 256
                                for kc in range(4):
                                    ins = e.matmul(
                                        pE[:, kc * 256:(kc + 1) * 256],
                                        kTp[off:off + 64,
                                            s * 512 + kc * 128:s * 512 + kc * 128 + 128],
                                        qTp[off:off + 64, q0:q0 + 256],
                                        start=True, stop=True)
                                ins.then_inc(sT.h, 1)
                            ph.on("tensor", smm); sT.n += 1
                            s_at = sT.n
                            def pmm(e, h=h, s=s, qh=qh, off=off, uu=uu,
                                    prev=(expm_hist.get(uu - 2))):
                                e.wait_ge(sG.h, pm_at)
                                if prev is not None:
                                    e.wait_ge(sA.h, prev)
                                q0 = s * 512 + qh * 256
                                pmoff = (2 + uu % 2) * 512
                                e.matmul(pB[0:16, pmoff:pmoff + 256],
                                         pmk[off:off + 64, h * PM:(h + 1) * PM],
                                         qTp[off:off + 64, q0:q0 + 256],
                                         start=True, stop=True).then_inc(sT.h, 1)
                            ph.on("tensor", pmm); sT.n += 1
                            pm_mm = sT.n
                            if pend_av:
                                ua, af = pend_av.pop(0)
                                ph.on("tensor", af); sT.n += 1
                                av_b[ua] = sT.n
                            def ex(e, pE=pE, s_at=s_at, uu=uu):
                                e.wait_ge(sT.h, s_at)
                                if uu >= 2 and (uu - 2) in av_b:
                                    e.wait_ge(sT.h, av_b[uu - 2])
                                e.activation(ETb[uu % 2][:], pE, AF.Exp,
                                             scale=1.0 / 8.0).then_inc(sA.h, 1)
                            ph.on("scalar", ex); sA.n += 1
                            def expm(e, pm_mm=pm_mm, uu=uu):
                                e.wait_ge(sT.h, pm_mm)
                                if uu >= 2 and (uu - 2) in av_b:
                                    e.wait_ge(sT.h, av_b[uu - 2])
                                pmoff = (2 + uu % 2) * 512
                                e.activation(ETpmb[uu % 2][:],
                                             pB[0:16, pmoff:pmoff + 256],
                                             AF.Exp, scale=1.0 / 8.0)
                                e.drain().then_inc(sA.h, 1)
                            ph.on("scalar", expm); sA.n += 1
                            expm_at = sA.n
                            expm_hist[uu] = sA.n
                            exp_done[uu] = sA.n
                            if pend_rec:
                                ur, rf = pend_rec.pop(0)
                                ph.on("vector", rf); sV.n += 1
                                ao_b[ur] = sV.n
                            def mk(e, a0=expm_at, uu=uu, qh=qh):
                                e.wait_ge(sA.h, a0)
                                e.wait_ge(sC.h, mask_at)
                                ev = ETb[uu % 2][:].rearrange("p (kc q) -> p kc q", q=256)
                                mv = mask[:].rearrange("p (kc q) -> p kc q", q=512)[
                                    :, :, qh * 256:(qh + 1) * 256]
                                e.tensor_mul(ev, ev, mv)
                                e.drain().then_inc(sV.h, 1)
                            ph.on("vector", mk); sV.n += 1
                            mk_at = sV.n
                            def av(e, h=h, s=s, qh=qh, uu=uu, m_at=mk_at, epm=expm_at):
                                e.wait_ge(sV.h, m_at)
                                e.wait_ge(sA.h, epm)
                                if uu >= 2 and (uu - 2) in ao_b:
                                    e.wait_ge(sV.h, ao_b[uu - 2])
                                po = (uu % 2) * 512
                                for kc in range(4):
                                    gt = s * 4 + kc
                                    for mc in range(2):
                                        e.matmul(
                                            pB[:, po + mc * 128:po + mc * 128 + 65],
                                            ETb[uu % 2][:, kc * 256 + mc * 128:
                                                        kc * 256 + mc * 128 + 128],
                                            vcat[:, gt * 1040 + h * 65:gt * 1040 + h * 65 + 65],
                                            start=(kc == 0 and mc == 0), stop=False,
                                            skip_group_check=True)
                                for mc in range(2):
                                    ins = e.matmul(
                                        pB[:, po + mc * 128:po + mc * 128 + 65],
                                        ETpmb[uu % 2][:, mc * 128:mc * 128 + 128],
                                        pmv[:, h * 65:(h + 1) * 65],
                                        start=False, stop=(mc == 1),
                                        skip_group_check=True)
                                ins.then_inc(sT.h, 1)
                            pend_av.append((uu, av))
                            def recev(e, uu=uu, h=h, s=s, qh=qh):
                                e.wait_ge(sT.h, av_b[uu])
                                po = (uu % 2) * 512
                                rc = (uu % 2) * 4
                                for mc in range(2):
                                    e.reciprocal(rcol[:, rc + mc:rc + mc + 1],
                                                 pB[:, po + mc * 128 + 64:po + mc * 128 + 65])
                                e.drain()
                                for mc in range(2):
                                    t = s * 4 + qh * 2 + mc
                                    e.tensor_mul(
                                        AO[:, t * 1024 + h * 64:t * 1024 + h * 64 + 64],
                                        pB[:, po + mc * 128:po + mc * 128 + 64],
                                        rcol[:, rc + mc:rc + mc + 1].broadcast_to([128, 64]))
                                e.drain().then_inc(sV.h, 1)
                            pend_rec.append((uu, recev))
            while pend_av:
                ua, af = pend_av.pop(0)
                ph.on("tensor", af); sT.n += 1
                av_b[ua] = sT.n
            while pend_rec:
                ur, rf = pend_rec.pop(0)
                ph.on("vector", rf); sV.n += 1
                ao_b[ur] = sV.n
            ph.emit()
            if stop <= 4:
                emit_dump("d_ao", AO)
                gR3.close(); gAO.close(); gL.close()
                return
            gR3.close()

            gR4 = contextlib.ExitStack()
            wres = gR4.enter_context(nc.sbuf_tensor("b_wro", [128, ND * 1024], BF16, side="right"))
            bro = gR4.enter_context(nc.sbuf_tensor("b_bro", [1, D], BF16, side="right"))
            ph = Phase(nc, "b_wo")
            ph.on("gpsimd", lambda e: e.dma_start(bro[:], dr["b_bout"]).then_inc(sG.h, 16))
            sG.n += 16
            bro_at = sG.n
            def wo(e):
                for kc in range(ND):
                    e.dma_start(wres[:, kc * 1024:(kc + 1) * 1024],
                                dr["b_wout"][kc]).then_inc(sG.h, 16)
            ph.on("gpsimd", wo); sG.n += 16 * ND
            wo_at = sG.n
            ph.emit()
            st_ret = emit_out_residual(
                "b_out", AO,
                rhs_fn=lambda kc, n: wres[:, kc * 1024 + n * 512:kc * 1024 + n * 512 + 512],
                nK=ND, brow_info=(bro, sG, bro_at), scale=1.0,
                tiles=list(range(NT)), target="X", accum=True,
                wtensor_at=(sG, wo_at), ln_stats=False)
            gR4.close()
            gAO.close()
            gL.close()
            return st_ret

        def emit_dbg(i):
            if not debug:
                return
            ph = Phase(nc, f"dbg{i}")
            tgt = sO.n + 16
            def d(e, tgt=tgt):
                e.dma_start(dbg[i].rearrange("(t p) d -> p t d", p=128),
                            X[:].rearrange("p (t d) -> p t d", t=NT)).then_inc(sO.h, 16)
                e.wait_ge(sO.h, tgt)
            ph.on("gpsimd", d); sO.n += 16
            ph.emit()

        bstop = 99
        if isinstance(n_sub, tuple):
            n_sub, bstop = n_sub
        if n_sub == 0:
            emit_attn_a(only_ln=True)
        else:
            st = emit_attn_a()
        emit_dbg(1)
        if n_sub >= 2:
            st = emit_ffn(1, last=(n_sub == 2), stats_in=st)
            emit_dbg(2)
        if n_sub >= 3:
            st = emit_attn_b(st, stop=bstop)
            emit_dbg(3)
        if n_sub >= 4:
            emit_ffn(2, last=True, stats_in=st)

        ph = Phase(nc, "fin")
        if n_sub == 0:
            fin0 = sO.n
            ph.on("sync", lambda e: e.wait_ge(sO.h, fin0))
        elif n_sub in (2, 4):
            fin = sO.n
            ph.on("sync", lambda e: e.wait_ge(sO.h, fin))
        else:
            fin2 = sO.n + 16
            def d(e, fin2=fin2):
                e.dma_start(y.rearrange("(t p) d -> p t d", p=128),
                            X[:].rearrange("p (t d) -> p t d", t=NT)).then_inc(sO.h, 16)
                e.wait_ge(sO.h, fin2)
            ph.on("gpsimd", d); sO.n += 16
        ph.emit()

    return nc


def kernel(**inputs):
    hp = host_prep(inputs)
    xs = np.asarray(inputs["L0_attn_a_x"], np.float32)
    nc = build_program(n_sub=4, debug=False)
    in_maps = [dict(hp, x=np.ascontiguousarray(xs[b])) for b in range(B)]
    res = run_bass_kernel_spmd(nc, in_maps, core_ids=list(range(B)))
    return np.stack([res.results[b]["y"] for b in range(B)], axis=0)


# version 19
# speedup vs baseline: 1.0116x; 1.0116x over previous
import contextlib
import numpy as np
import ml_dtypes
import concourse.bass as bass
import concourse.mybir as mybir
from concourse.bass_utils import run_bass_kernel_spmd

F32 = mybir.dt.float32
BF16 = mybir.dt.bfloat16
I32 = mybir.dt.int32
AF = mybir.ActivationFunctionType
ALU = mybir.AluOpType

B, S, D = 8, 2048, 1024
H, Hd = 16, 64
HID = 3072
SEG, PM = 512, 16
NT, ND, NHC, NSEG = S // 128, D // 128, HID // 128, S // SEG

BETA = 0.95
_a4 = 1 + BETA * (1 + BETA * (1 + BETA))
TH = (1.0 / _a4, (1 + BETA) / _a4, (2 + BETA) / _a4, (2 + BETA + BETA**2) / _a4)
DQ1 = 1.0 / 127.0
DQ2 = 1.0 / (127.0 * 127.0)


def _deint_perm():
    p = []
    for h in range(H):
        p += [h * 64 + 2 * i for i in range(32)]
        p += [h * 64 + 2 * i + 1 for i in range(32)]
    return np.array(p, dtype=np.int64)


def host_prep(ins):
    c = np.ascontiguousarray
    o = {}

    def lhsT_tiles(w):
        return c(w.reshape(ND, 128, ND, 128).transpose(2, 1, 0, 3).reshape(ND, 128, D))

    o["a_qw"] = lhsT_tiles(np.asarray(ins["L0_attn_a_q_w"], np.float32))
    o["a_kw"] = lhsT_tiles(np.asarray(ins["L0_attn_a_k_w"], np.float32))
    o["a_vw"] = c(np.asarray(ins["L0_attn_a_v_w"], np.float32).reshape(ND, 128, D))
    o["a_ow"] = c(np.asarray(ins["L0_attn_a_o_w"], np.float32).reshape(ND, 128, D))
    o["a_qb"] = c(np.asarray(ins["L0_attn_a_q_b"], np.float32).reshape(ND, 128).T)
    o["a_kb"] = c(np.asarray(ins["L0_attn_a_k_b"], np.float32).reshape(ND, 128).T)
    o["a_vb"] = c(np.asarray(ins["L0_attn_a_v_b"], np.float32).reshape(1, D))
    o["a_ob"] = c(np.asarray(ins["L0_attn_a_o_b"], np.float32).reshape(1, D))

    def ffn_lhsT(w):
        return c(w.reshape(NHC, 128, ND, 128).transpose(0, 3, 2, 1).reshape(NHC, 128, D))
    o["f_wg"] = ffn_lhsT(np.asarray(ins["L0_ffn_w_gate"], np.int32))
    o["f_wu"] = ffn_lhsT(np.asarray(ins["L0_ffn_w_up"], np.int32))
    o["f_wd"] = c(np.asarray(ins["L0_ffn_w_down"], np.int32).T.reshape(NHC, 128, D))

    perm = _deint_perm()
    wqkv = np.asarray(ins["L0_attn_b_w_qkv"], np.float32).copy()
    wqkv[:, 0:D] = wqkv[:, 0:D][:, perm]
    wqkv[:, D:2 * D] = wqkv[:, D:2 * D][:, perm]
    o["b_wqkv"] = c(wqkv.reshape(ND, 128, 6, 512).transpose(2, 0, 1, 3))
    o["b_nw"] = c(np.asarray(ins["L0_attn_b_norm_w"], np.float32).reshape(ND, 128).T)
    pmk = np.asarray(ins["L0_attn_b_pm_k"], np.float32)[:, :, perm[:64]]
    pmkT = pmk.transpose(2, 0, 1).reshape(64, H * PM)
    o["b_pmkT"] = c(np.concatenate([pmkT, pmkT], axis=0))
    pmv = np.asarray(ins["L0_attn_b_pm_v"], np.float32)
    pv = np.ones((PM, H, 65), np.float32)
    pv[:, :, 0:64] = pmv.transpose(1, 0, 2)
    o["b_pmv"] = c(pv.reshape(PM, H * 65))
    o["b_wout"] = c(np.asarray(ins["L0_attn_b_w_out"], np.float32).reshape(ND, 128, D))
    o["b_bout"] = c(np.asarray(ins["L0_attn_b_b_out"], np.float32).reshape(1, D))

    o["c_ident"] = np.eye(128, dtype=ml_dtypes.bfloat16)
    m = np.zeros((128, NSEG * 512), np.float32)
    for kc in range(4):
        q = np.arange(512)[None, :]
        p = np.arange(128)[:, None]
        m[:, kc * 512:(kc + 1) * 512] = (q >= kc * 128 + p)
    o["c_mask"] = m.astype(ml_dtypes.bfloat16)
    inv = (1.0 / 10000.0 ** (np.arange(0, Hd, 2, dtype=np.float32) / Hd)).astype(np.float32)
    fr = np.arange(S, dtype=np.float32)[:, None] * inv[None, :]
    cos32, sin32 = np.cos(fr), np.sin(fr)
    cosh = np.concatenate([cos32, cos32], axis=1)
    sinh = np.concatenate([-sin32, sin32], axis=1)
    o["c_cossin"] = np.concatenate(
        [np.tile(cosh, (1, H)), np.tile(sinh, (1, H))], axis=1).astype(ml_dtypes.bfloat16)
    return o


class Sem:
    def __init__(self, stack, nc, name):
        self.h = stack.enter_context(nc.semaphore(name))
        self.n = 0


class Phase:
    ENGS = ("sync", "gpsimd", "tensor", "vector", "scalar")

    def __init__(self, nc, name):
        self.nc, self.name = nc, name
        self.ops = {e: [] for e in self.ENGS}

    def on(self, eng, fn):
        self.ops[eng].append(fn)

    def emit(self):
        with self.nc.Block(self.name) as blk:
            for eng in self.ENGS:
                fns = self.ops[eng]
                if fns:
                    getattr(blk, eng)(lambda e, fns=fns: [f(e) for f in fns])


def build_program(n_sub=4, debug=False):
    nc = bass.Bass("TRN2", target_bir_lowering=False)
    gs = contextlib.ExitStack()

    dr = {}
    def din(name, shape, dt):
        dr[name] = nc.dram_tensor(name, shape, dt, kind="ExternalInput").ap()
    din("x", [S, D], F32)
    din("a_qw", [ND, 128, D], F32); din("a_kw", [ND, 128, D], F32)
    din("a_vw", [ND, 128, D], F32); din("a_ow", [ND, 128, D], F32)
    din("a_qb", [128, ND], F32); din("a_kb", [128, ND], F32)
    din("a_vb", [1, D], F32); din("a_ob", [1, D], F32)
    din("f_wg", [NHC, 128, D], I32); din("f_wu", [NHC, 128, D], I32)
    din("f_wd", [NHC, 128, D], I32)
    din("b_wqkv", [6, ND, 128, 512], F32)
    din("b_nw", [128, ND], F32)
    din("b_pmkT", [128, H * PM], F32)
    din("b_pmv", [PM, H * 65], F32)
    din("b_wout", [ND, 128, D], F32)
    din("b_bout", [1, D], F32)
    din("c_ident", [128, 128], BF16)
    din("c_mask", [128, NSEG * 512], BF16)
    din("c_cossin", [S, 2048], BF16)
    y = nc.dram_tensor("y", [S, D], F32, kind="ExternalOutput").ap()
    dbg = {}
    if debug:
        for i in range(1, 4):
            dbg[i] = nc.dram_tensor(f"dbg{i}", [S, D], F32, kind="ExternalOutput").ap()
        for nm, cols in (("d_xT", ND * 2048), ("d_q", ND * 2048), ("d_k", ND * 2048),
                         ("d_v", NT * 1040), ("d_ao", NT * 1024)):
            dbg[nm] = nc.dram_tensor(nm, [128, cols], F32, kind="ExternalOutput").ap()

    with gs:
        sX = Sem(gs, nc, "sX"); sC = Sem(gs, nc, "sC"); sG = Sem(gs, nc, "sG")
        sT = Sem(gs, nc, "sT"); sV = Sem(gs, nc, "sV"); sA = Sem(gs, nc, "sA")
        sO = Sem(gs, nc, "sO")

        def ga(name, shape, dt, side="left"):
            return gs.enter_context(nc.sbuf_tensor(name, shape, dt, side=side))

        X = ga("X", [128, NT * 1024], BF16)
        junk = ga("junk", [128, 1024], BF16)
        ident = ga("ident", [128, 128], BF16)
        onesr = ga("onesr", [1, 128], BF16)
        sums = ga("sums", [128, NT], F32)
        sumsq = ga("sumsq", [128, NT], F32)
        mcol = ga("mcol", [128, NT], F32)
        vcol = ga("vcol", [128, NT], F32)
        tcol = ga("tcol", [128, NT], F32)
        scol = ga("scol", [128, NT], F32)
        bcol = ga("bcol", [128, NT], F32)
        rcol = ga("rcol", [128, 8], F32)
        neg4 = ga("neg4", [128, 1], F32)
        aot = [ga(f"aot{i}", [128, 1024], BF16) for i in range(2)]
        pA = gs.enter_context(nc.psum_tensor("pA", [128, 2048], F32))
        pB = gs.enter_context(nc.psum_tensor("pB", [128, 2048], F32))

        Xt = lambda t: X[:, t * 1024:(t + 1) * 1024]

        ph = Phase(nc, "load")
        def ld_x(e):
            for hf in range(4):
                e.dma_start(
                    X[:, hf * 4096:(hf + 1) * 4096].rearrange("p (t d) -> p t d", t=4),
                    dr["x"][hf * 512:(hf + 1) * 512].rearrange(
                        "(t p) d -> p t d", p=128)).then_inc(sX.h, 16)
        ph.on("gpsimd", ld_x); sX.n += 64
        ph.on("sync", lambda e: e.dma_start(ident[:], dr["c_ident"]).then_inc(sC.h, 16))
        sC.n += 16
        IDENT_AT = sC.n
        ph.on("vector", lambda e: e.memset(onesr[:], 1.0))
        ph.on("vector", lambda e: e.memset(neg4[:], -4.0))
        ph.emit()

        dump_ctr = [0]
        def emit_dump(nm, src_sb):
            if not debug:
                return
            dump_ctr[0] += 1
            ph = Phase(nc, f"dump{dump_ctr[0]}_" + nm)
            tgt = sO.n + 16
            def d(e, tgt=tgt):
                e.dma_start(dbg[nm], src_sb[:]).then_inc(sO.h, 16)
                e.wait_ge(sO.h, tgt)
            ph.on("gpsimd", d); sO.n += 16
            ph.emit()

        def emit_ln_stats(ph, have_sums, rms_fold, have_sumsq=False):
            eps_mul, eps_add = (1.0 + 1e-6, 1e-11) if rms_fold else (1.0, 1e-5)
            def v1(e):
                if not have_sumsq:
                    for t in range(NT):
                        if not have_sums and t % 4 == 0:
                            e.wait_ge(sX.h, 16 * (t // 4 + 1))
                        if not have_sums:
                            e.tensor_scalar(junk[:], Xt(t), 0.0, None, ALU.add, ALU.add,
                                            accum_out=sums[:, t:t + 1])
                        e.scalar_tensor_tensor(junk[:], Xt(t), 1.0, Xt(t), ALU.mult,
                                               ALU.mult, accum_out=sumsq[:, t:t + 1])
                e.drain()
                e.tensor_scalar(mcol[:], sums[:], 1.0 / D, None, ALU.mult)
                e.tensor_scalar(vcol[:], sumsq[:], 1.0 / D, None, ALU.mult)
                e.drain()
                e.scalar_tensor_tensor(tcol[:], mcol[:], 1.0, mcol[:], ALU.mult, ALU.mult)
                e.drain()
                e.tensor_sub(vcol[:], vcol[:], tcol[:])
                e.drain()
                e.tensor_scalar(vcol[:], vcol[:], eps_mul, eps_add, ALU.mult,
                                ALU.add)
                e.drain().then_inc(sV.h, 1)
            ph.on("vector", v1); sV.n += 1
            v1_at = sV.n
            def s1(e, v1_at=v1_at):
                e.wait_ge(sV.h, v1_at)
                e.activation(vcol[:], vcol[:], AF.Sqrt)
                e.drain().then_inc(sA.h, 1)
            ph.on("scalar", s1); sA.n += 1
            s1_at = sA.n
            def v2(e, s1_at=s1_at):
                e.wait_ge(sA.h, s1_at)
                e.reciprocal(scol[:], vcol[:])
                e.tensor_scalar(bcol[:], mcol[:], -1.0, None, ALU.mult)
                e.drain()
                e.tensor_mul(bcol[:], bcol[:], scol[:])
                e.drain().then_inc(sV.h, 1)
            ph.on("vector", v2); sV.n += 1
            return sV.n

        def emit_lnT(ph, tiles, destT, dest_cols, stats_at, xtiles, nw=None):
            n = len(tiles)
            ap_at, ev_at, mm_at = {}, {}, {}

            def mk_apply(i):
                t, par = tiles[i], i % 2
                def ap(e, t=t, par=par):
                    e.wait_ge(sV.h, stats_at)
                    e.activation(xtiles[par][:], Xt(t), AF.Identity,
                                 bias=bcol[:, t:t + 1],
                                 scale=scol[:, t:t + 1])
                    e.drain().then_inc(sA.h, 1)
                ph.on("scalar", ap); sA.n += 1
                ap_at[i] = sA.n

            def mk_tmm(i):
                par = i % 2
                def tmm(e, i=i, par=par):
                    e.wait_ge(sA.h, ap_at[i])
                    e.wait_ge(sC.h, IDENT_AT)
                    if i >= 2:
                        e.wait_ge(sV.h, ev_at[i - 2])
                    for dc in range(ND):
                        bank = par * 2 + dc // 4
                        off = bank * 512 + (dc % 4) * 128
                        ins = e.matmul(pB[:, off:off + 128],
                                       xtiles[par][:, dc * 128:(dc + 1) * 128],
                                       ident[:], start=True, stop=True)
                    ins.then_inc(sT.h, 1)
                ph.on("tensor", tmm); sT.n += 1
                mm_at[i] = sT.n

            def mk_evac(i):
                par = i % 2
                def ev(e, i=i, par=par):
                    e.wait_ge(sT.h, mm_at[i])
                    if nw is None:
                        for g in range(2):
                            src = pB[:, (par * 2 + g) * 512:(par * 2 + g + 1) * 512]
                            dv = destT[:].rearrange("p (c u) -> p c u", u=dest_cols)[
                                :, g * 4:(g + 1) * 4, i * 128:i * 128 + 128]
                            e.tensor_copy(dv, src)
                    else:
                        for dc in range(ND):
                            bank = par * 2 + dc // 4
                            off = bank * 512 + (dc % 4) * 128
                            dv = destT[:, dc * dest_cols + i * 128:
                                       dc * dest_cols + i * 128 + 128]
                            e.tensor_mul(dv, pB[:, off:off + 128],
                                         nw[:, dc:dc + 1].broadcast_to([128, 128]))
                    e.drain().then_inc(sV.h, 1)
                ph.on("vector", ev); sV.n += 1
                ev_at[i] = sV.n

            mk_apply(0)
            if n > 1:
                mk_apply(1)
            for i in range(n):
                mk_tmm(i)
                mk_evac(i)
                if i + 2 < n:
                    mk_apply(i + 2)
            return sA.n

        def emit_out_residual(phname, src_AO, rhs_fn, nK, brow_info, scale, tiles,
                              target, accum, wtensor_at, transpose_src=True,
                              lhsT_fn=None, ln_stats=None):
            ph = Phase(nc, phname)
            ev_hist, tr_ev, prj_at = {}, {}, {}
            st_at = {}
            n = len(tiles)

            def mk_tr(i):
                t, par = tiles[i], i % 2
                def tmm(e, i=i, t=t, par=par):
                    e.wait_ge(sC.h, IDENT_AT)
                    if i >= 2:
                        e.wait_ge(sA.h, tr_ev[i - 2])
                    for dc in range(ND):
                        bank = par * 2 + dc // 4
                        off = bank * 512 + (dc % 4) * 128
                        ins = e.matmul(pB[:, off:off + 128],
                                       src_AO[:, t * 1024 + dc * 128:t * 1024 + dc * 128 + 128],
                                       ident[:], start=True, stop=True)
                    ins.then_inc(sT.h, 1)
                ph.on("tensor", tmm); sT.n += 1
                mm = sT.n
                def ev(e, i=i, par=par, mm=mm):
                    e.wait_ge(sT.h, mm)
                    if i >= 2:
                        e.wait_ge(sT.h, prj_at[i - 2])
                    for g in range(2):
                        e.activation(aot[par][:, g * 512:(g + 1) * 512],
                                     pB[:, (par * 2 + g) * 512:(par * 2 + g + 1) * 512],
                                     AF.Identity)
                    e.drain().then_inc(sA.h, 1)
                ph.on("scalar", ev); sA.n += 1
                tr_ev[i] = sA.n

            def mk_prj(i):
                t, par = tiles[i], i % 2
                psum = pA[:, par * 1024:(par + 1) * 1024]
                def mm(e, i=i, t=t, par=par, psum=psum):
                    e.wait_ge(wtensor_at[0].h, wtensor_at[1])
                    if transpose_src:
                        e.wait_ge(sA.h, tr_ev[i])
                    if i >= 2:
                        e.wait_ge(sV.h, ev_hist[i - 2])
                    for nn in range(2):
                        for kc in range(nK):
                            lh = (aot[par][:, kc * 128:(kc + 1) * 128]
                                  if transpose_src else lhsT_fn(t, kc))
                            ins = e.matmul(psum[:, nn * 512:(nn + 1) * 512], lh,
                                           rhs_fn(kc, nn), start=(kc == 0),
                                           stop=(kc == nK - 1 and brow_info is None))
                        if brow_info is not None:
                            bw, bsem, bat = brow_info
                            e.wait_ge(bsem.h, bat)
                            ins = e.matmul(psum[:, nn * 512:(nn + 1) * 512],
                                           onesr[:], bw[:, nn * 512:(nn + 1) * 512],
                                           start=False, stop=True)
                    ins.then_inc(sT.h, 1)
                ph.on("tensor", mm); sT.n += 1
                prj_at[i] = sT.n
                def ev(e, i=i, t=t, par=par, psum=psum, mmv=sT.n):
                    e.wait_ge(sT.h, mmv)
                    if target == "out" and i >= 2:
                        e.wait_ge(sO.h, st_at[i - 2])
                    dst = Xt(t) if target == "X" else outst_buf[
                        :, (i % 2) * 1024:(i % 2 + 1) * 1024]
                    kw = {"accum_out": sums[:, t:t + 1]} if accum else {}
                    e.scalar_tensor_tensor(dst, psum[:], scale, Xt(t),
                                           ALU.mult, ALU.add, **kw)
                    if ln_stats is not None:
                        e.drain()
                        e.scalar_tensor_tensor(junk[:], Xt(t), 1.0, Xt(t), ALU.mult,
                                               ALU.mult, accum_out=sumsq[:, t:t + 1])
                    e.drain().then_inc(sV.h, 1)
                ph.on("vector", ev); sV.n += 1
                ev_hist[i] = sV.n
                if target == "out":
                    def st(e, t=t, i=i, v_at=sV.n):
                        e.wait_ge(sV.h, v_at)
                        e.dma_start(y[t * 128:(t + 1) * 128, :],
                                    outst_buf[:, (i % 2) * 1024:(i % 2 + 1) * 1024]
                                    ).then_inc(sO.h, 16)
                    ph.on("sync", st); sO.n += 16
                    st_at[i] = sO.n

            if transpose_src:
                mk_tr(0)
                for i in range(n):
                    if i + 1 < n:
                        mk_tr(i + 1)
                    mk_prj(i)
            else:
                for i in range(n):
                    mk_prj(i)
            st_ret = None
            if ln_stats is not None and ln_stats != "inline":
                st_ret = emit_ln_stats(ph, have_sums=True, rms_fold=ln_stats,
                                       have_sumsq=True)
            ph.emit()
            return st_ret

        def emit_attn_a(only_ln=False):
            gL = contextlib.ExitStack()
            gR = contextlib.ExitStack()
            xT = gR.enter_context(nc.sbuf_tensor("a_xT", [128, ND * 2048], BF16, side="right"))
            xtiles = [gR.enter_context(nc.sbuf_tensor(f"a_xt{i}", [128, 1024], BF16, side="right"))
                      for i in range(2)]
            qaggT = gL.enter_context(nc.sbuf_tensor("qaggT", [128, ND * 2048], BF16))
            kaggT = gL.enter_context(nc.sbuf_tensor("kaggT", [128, ND * 2048], BF16))
            vcat = gL.enter_context(nc.sbuf_tensor("a_vcat", [128, NT * 1040], BF16))

            gV = contextlib.ExitStack()
            bvr = gV.enter_context(nc.sbuf_tensor("bvr", [1, D], BF16, side="right"))
            wres_v = gV.enter_context(nc.sbuf_tensor("a_wrv", [128, ND * 1024], BF16, side="right"))
            ph = Phase(nc, "a_ln1")
            st_at = emit_ln_stats(ph, have_sums=False, rms_fold=False)
            emit_lnT(ph, list(range(NT)), xT, 2048, st_at, xtiles)
            def vpre(e):
                e.dma_start(bvr[:], dr["a_vb"]).then_inc(sG.h, 16)
                for b in range(2):
                    e.dma_start(
                        wres_v[:, b * 4096:(b + 1) * 4096].rearrange(
                            "p (c m) -> p c m", c=4),
                        dr["a_vw"][b * 4:(b + 1) * 4].transpose([1, 0, 2])
                    ).then_inc(sG.h, 16)
            ph.on("gpsimd", vpre); sG.n += 48
            vload_at = sG.n
            ph.emit()
            if only_ln:
                emit_dump("d_xT", xT)
                gR.close()
                gL.close()
                return

            cbuf = [gV.enter_context(nc.sbuf_tensor(f"vcb{i}", [128, 1024], F32, side="right"))
                    for i in range(2)]
            acc = gV.enter_context(nc.sbuf_tensor("vacc", [128, 1024], F32, side="right"))
            ph = Phase(nc, "a_v")
            def vc_ones(e):
                e.memset(vcat[:].rearrange("p (t h c) -> p t h c", t=NT, c=65)[
                    :, :, :, 64:65], 1.0)
                e.drain().then_inc(sV.h, 1)
            ph.on("vector", vc_ones); sV.n += 1
            ev2, lif2 = {}, {}
            for t in range(NT):
                par = t % 2
                psum = pA[:, par * 1024:(par + 1) * 1024]
                def mm(e, t=t, psum=psum):
                    e.wait_ge(sG.h, vload_at)
                    if t >= 2:
                        e.wait_ge(sA.h, ev2[t - 2])
                    for n in range(2):
                        for kc in range(ND):
                            e.matmul(psum[:, n * 512:(n + 1) * 512],
                                     xT[:, kc * 2048 + t * 128:kc * 2048 + t * 128 + 128],
                                     wres_v[:, kc * 1024 + n * 512:kc * 1024 + n * 512 + 512],
                                     start=(kc == 0), stop=False)
                        ins = e.matmul(psum[:, n * 512:(n + 1) * 512], onesr[:],
                                       bvr[:, n * 512:(n + 1) * 512],
                                       start=False, stop=True)
                    ins.then_inc(sT.h, 1)
                ph.on("tensor", mm); sT.n += 1
                mm_at = sT.n
                def ev(e, par=par, psum=psum, mm_at=mm_at, t=t):
                    e.wait_ge(sT.h, mm_at)
                    if t >= 2:
                        e.wait_ge(sV.h, lif2[t - 2])
                    e.activation(cbuf[par][:], psum, AF.Identity)
                    e.drain().then_inc(sA.h, 1)
                ph.on("scalar", ev); sA.n += 1
                ev2[t] = sA.n
                def lif(e, t=t, par=par, a_at=sA.n):
                    e.wait_ge(sA.h, a_at)
                    cb3 = cbuf[par][:].rearrange("p (h c) -> p h c", c=64)
                    a3 = acc[:].rearrange("p (h c) -> p h c", c=64)
                    cb = cbuf[par][:]
                    a = acc[:]
                    e.tensor_scalar(a, cb, TH[0], None, ALU.is_gt)
                    e.scalar_tensor_tensor(a, cb, TH[1], a, ALU.is_gt, ALU.add)
                    e.scalar_tensor_tensor(a, cb, TH[2], a, ALU.is_gt, ALU.add)
                    dstv = vcat[:].rearrange("p (u h c) -> p u h c", u=NT, c=65)[
                        :, t, :, 0:64]
                    e.scalar_tensor_tensor(dstv, cb3, TH[3], a3, ALU.is_gt,
                                           ALU.add)
                    e.drain().then_inc(sV.h, 1)
                ph.on("vector", lif); sV.n += 1
                lif2[t] = sV.n
            ph.emit()
            gV.close()

            gQ = contextlib.ExitStack()
            wring = [gQ.enter_context(nc.sbuf_tensor(f"awr{i}", [128, 2048], BF16, side="right"))
                     for i in range(3)]
            bq = gQ.enter_context(nc.sbuf_tensor("bq", [128, ND], F32, side="right"))
            bk = gQ.enter_context(nc.sbuf_tensor("bk", [128, ND], F32, side="right"))
            cbuf = [gQ.enter_context(nc.sbuf_tensor(f"qcb{i}", [128, 1024], F32, side="right"))
                    for i in range(2)]
            acc = gQ.enter_context(nc.sbuf_tensor("qacc", [128, 1024], F32, side="right"))
            ph = Phase(nc, "a_qk")
            ph.on("sync", lambda e: e.dma_start(bq[:], dr["a_qb"]).then_inc(sC.h, 16))
            sC.n += 16
            ph.on("sync", lambda e: e.dma_start(bk[:], dr["a_kb"]).then_inc(sC.h, 16))
            sC.n += 16
            bias_at = sC.n
            slabs = [("a_qw", dc) for dc in range(ND)] + [("a_kw", dc) for dc in range(ND)]
            pairs = [("a_qw", 2 * p) for p in range(4)] + [("a_kw", 2 * p) for p in range(4)]
            pair_at = [None] * len(pairs)
            def ld_pair(e, jp, slot):
                wn, d0 = pairs[jp]
                e.dma_start(wring[slot][:].rearrange("p (c m) -> p c m", c=2),
                            dr[wn][d0:d0 + 2].transpose([1, 0, 2])).then_inc(sG.h, 16)
            for jp in range(3):
                ph.on("gpsimd", lambda e, jp=jp: ld_pair(e, jp, jp % 3))
                sG.n += 16
                pair_at[jp] = sG.n
            ev_hist, lif_hist = {}, {}
            iters = [(j, n2) for j in range(len(slabs)) for n2 in range(2)]
            for it, (j, n2) in enumerate(iters):
                wn, dc = slabs[j]
                jp, par = j // 2, it % 4
                slot = jp % 3
                scol0 = (j % 2) * 1024
                psum = (pA if par < 2 else pB)[:, (par % 2) * 1024:(par % 2 + 1) * 1024]
                dst = qaggT if wn == "a_qw" else kaggT
                bias = bq if wn == "a_qw" else bk
                def mm(e, it=it, j=j, n2=n2, slot=slot, scol0=scol0, psum=psum,
                       g_at=pair_at[jp]):
                    e.wait_ge(sG.h, g_at)
                    if it >= 2:
                        e.wait_ge(sA.h, ev_hist[it - 2])
                    for n in range(2):
                        for kc in range(ND):
                            col = n2 * 1024 + n * 512
                            ins = e.matmul(
                                psum[:, n * 512:(n + 1) * 512],
                                wring[slot][:, scol0 + kc * 128:scol0 + (kc + 1) * 128],
                                xT[:, kc * 2048 + col:kc * 2048 + col + 512],
                                start=(kc == 0), stop=(kc == ND - 1))
                    ins.then_inc(sT.h, 1)
                ph.on("tensor", mm); sT.n += 1
                mm_at = sT.n
                if n2 == 1 and j % 2 == 1 and jp + 3 < len(pairs):
                    def pre(e, jp=jp, slot=slot, mm_at=mm_at):
                        e.wait_ge(sT.h, mm_at)
                        ld_pair(e, jp + 3, slot)
                    ph.on("gpsimd", pre); sG.n += 16
                    pair_at[jp + 3] = sG.n
                def ev(e, it=it, psum=psum, bias=bias, dc=dc, mm_at=mm_at):
                    e.wait_ge(sT.h, mm_at)
                    e.wait_ge(sC.h, bias_at)
                    if it >= 2:
                        e.wait_ge(sV.h, lif_hist[it - 2])
                    e.activation(cbuf[it % 2][:], psum[:], AF.Identity,
                                 bias=bias[:, dc:dc + 1])
                    e.drain().then_inc(sA.h, 1)
                ph.on("scalar", ev); sA.n += 1
                ev_hist[it] = sA.n
                def lif(e, it=it, n2=n2, dst=dst, dc=dc, a_at=sA.n):
                    e.wait_ge(sA.h, a_at)
                    cb = cbuf[it % 2]
                    e.tensor_scalar(acc[:], cb[:], TH[0], None, ALU.is_gt)
                    e.scalar_tensor_tensor(acc[:], cb[:], TH[1], acc[:], ALU.is_gt, ALU.add)
                    e.scalar_tensor_tensor(acc[:], cb[:], TH[2], acc[:], ALU.is_gt, ALU.add)
                    e.scalar_tensor_tensor(
                        dst[:, dc * 2048 + n2 * 1024:dc * 2048 + n2 * 1024 + 1024],
                        cb[:], TH[3], acc[:], ALU.is_gt, ALU.add)
                    e.drain().then_inc(sV.h, 1)
                ph.on("vector", lif); sV.n += 1
                lif_hist[it] = sV.n
            ph.emit()
            emit_dump("d_xT", xT)
            emit_dump("d_q", qaggT)
            emit_dump("d_k", kaggT)
            emit_dump("d_v", vcat)
            gQ.close()
            gR.close()

            g4 = contextlib.ExitStack()
            AO = g4.enter_context(nc.sbuf_tensor("a_AO", [128, NT * 1024], BF16))
            bor = g4.enter_context(nc.sbuf_tensor("bor", [1, D], BF16))
            gE = contextlib.ExitStack()
            ETb = [gE.enter_context(nc.sbuf_tensor(f"a_ET{i}", [128, 16 * 512], BF16))
                   for i in range(2)]

            ph = Phase(nc, "a_sc")
            ph.on("gpsimd", lambda e: e.dma_start(bor[:], dr["a_ob"]).then_inc(sG.h, 16))
            sG.n += 16
            bor_at = sG.n

            exp_hist, av_done, ao_ev = {}, {}, {}
            pend_sc, pend_av = [], []
            GRP = [(0, 4), (4, 3), (7, 4), (11, 3), (14, 2)]
            GP = [0, 1, 0, 1, 0]
            units = [(h, qb) for h in range(H) for qb in range(4)]
            for u, (h, qb) in enumerate(units):
                hc, off = h // 2, (h % 2) * 64
                for g, (kc0, nkc) in enumerate(GRP):
                    pP = pA[:, 0:2048] if GP[g] == 0 else pB[:, 0:1536]
                    def smm(e, u=u, g=g, kc0=kc0, nkc=nkc, hc=hc, off=off, qb=qb, pP=pP):
                        key = (u, g - 2) if g >= 2 else (u - 1, 4 if g == 0 else 3)
                        if key in exp_hist:
                            e.wait_ge(sA.h, exp_hist[key])
                        for k in range(nkc):
                            kc = kc0 + k
                            ins = e.matmul(
                                pP[:, k * 512:(k + 1) * 512],
                                kaggT[off:off + 64,
                                      hc * 2048 + kc * 128:hc * 2048 + kc * 128 + 128],
                                qaggT[off:off + 64,
                                      hc * 2048 + qb * 512:hc * 2048 + qb * 512 + 512],
                                start=True, stop=True)
                        ins.then_inc(sT.h, 1)
                    ph.on("tensor", smm); sT.n += 1
                    s_at = sT.n
                    def ex(e, u=u, g=g, kc0=kc0, nkc=nkc, pP=pP, s_at=s_at):
                        e.wait_ge(sT.h, s_at)
                        if u >= 2 and (u - 2) in av_done:
                            e.wait_ge(sT.h, av_done[u - 2])
                        ins = e.activation(ETb[u % 2][:, kc0 * 512:(kc0 + nkc) * 512],
                                           pP[:, 0:nkc * 512],
                                           AF.Exp, bias=neg4[:], scale=1.0 / 128.0)
                        if g == 4:
                            e.drain()
                        ins.then_inc(sA.h, 1)
                    ph.on("scalar", ex); sA.n += 1
                    exp_hist[(u, g)] = sA.n
                exp_all = sA.n
                if pend_av:
                    ua, af = pend_av.pop(0)
                    ph.on("tensor", af); sT.n += 1
                    av_done[ua] = sT.n
                if pend_sc:
                    up, scf = pend_sc.pop(0)
                    ph.on("vector", scf); sV.n += 1
                    ao_ev[up] = sV.n
                def avmm(e, u=u, h=h, exp_all=exp_all):
                    e.wait_ge(sA.h, exp_all)
                    if u >= 1:
                        e.wait_ge(sV.h, ao_ev[u - 1])
                    for kc in range(16):
                        for mc in range(4):
                            ins = e.matmul(
                                pB[:, 1536 + mc * 128:1536 + mc * 128 + 65],
                                ETb[u % 2][:, kc * 512 + mc * 128:kc * 512 + mc * 128 + 128],
                                vcat[:, kc * 1040 + h * 65:kc * 1040 + h * 65 + 65],
                                start=(kc == 0 and mc == 0),
                                stop=(kc == 15 and mc == 3),
                                skip_group_check=True)
                    ins.then_inc(sT.h, 1)
                pend_av.append((u, avmm))
                def recev(e, u=u, h=h, qb=qb):
                    e.wait_ge(sT.h, av_done[u])
                    rc = (u % 2) * 4
                    for mc in range(4):
                        e.reciprocal(rcol[:, rc + mc:rc + mc + 1],
                                     pB[:, 1536 + mc * 128 + 64:1536 + mc * 128 + 65])
                    e.drain()
                    e.tensor_scalar(rcol[:, rc:rc + 4], rcol[:, rc:rc + 4], 0.25,
                                    None, ALU.mult)
                    e.drain()
                    for mc in range(4):
                        t = qb * 4 + mc
                        e.tensor_mul(
                            AO[:, t * 1024 + h * 64:t * 1024 + h * 64 + 64],
                            pB[:, 1536 + mc * 128:1536 + mc * 128 + 64],
                            rcol[:, rc + mc:rc + mc + 1].broadcast_to([128, 64]))
                    e.drain().then_inc(sV.h, 1)
                pend_sc.append((u, recev))
            while pend_av:
                ua, af = pend_av.pop(0)
                ph.on("tensor", af); sT.n += 1
                av_done[ua] = sT.n
            while pend_sc:
                up, scf = pend_sc.pop(0)
                ph.on("vector", scf); sV.n += 1
                ao_ev[up] = sV.n
            ph.emit()
            gE.close()
            emit_dump("d_ao", AO)

            g5 = contextlib.ExitStack()
            wres = g5.enter_context(nc.sbuf_tensor("a_wro", [128, ND * 1024], BF16))
            ph = Phase(nc, "a_wo")
            def oslabs(e):
                for b in range(2):
                    e.dma_start(
                        wres[:, b * 4096:(b + 1) * 4096].rearrange(
                            "p (c m) -> p c m", c=4),
                        dr["a_ow"][b * 4:(b + 1) * 4].transpose([1, 0, 2])
                    ).then_inc(sG.h, 16)
            ph.on("gpsimd", oslabs); sG.n += 32
            oload_at = sG.n
            ph.emit()
            st_ret = emit_out_residual(
                "a_out", AO,
                rhs_fn=lambda kc, n: wres[:, kc * 1024 + n * 512:kc * 1024 + n * 512 + 512],
                nK=ND, brow_info=(bor, sG, bor_at), scale=1.0,
                tiles=list(range(NT)), target="X", accum=True,
                wtensor_at=(sG, oload_at), ln_stats=False)
            g5.close()
            g4.close()
            gL.close()
            return st_ret

        def emit_ffn(app, last, stats_in):
            gL = contextlib.ExitStack()
            gR = contextlib.ExitStack()
            hb = gL.enter_context(nc.sbuf_tensor(f"f{app}_hb", [128, NHC * 1024], BF16))
            wdT = gL.enter_context(nc.sbuf_tensor(f"f{app}_wdT", [128, NHC * 1024], BF16))
            global outst_buf
            if last:
                outst_buf = gL.enter_context(nc.sbuf_tensor(f"outst{app}", [128, 2048], F32))
            xTf = gR.enter_context(nc.sbuf_tensor(f"f{app}_xT", [128, ND * 1024], BF16, side="right"))
            xtiles = [gR.enter_context(nc.sbuf_tensor(f"f{app}_xt{i}", [128, 1024], BF16, side="right"))
                      for i in range(2)]
            wgr = [gR.enter_context(nc.sbuf_tensor(f"f{app}_wg{i}", [128, 2048], BF16, side="right"))
                   for i in range(2)]
            wur = [gR.enter_context(nc.sbuf_tensor(f"f{app}_wu{i}", [128, 2048], BF16, side="right"))
                   for i in range(2)]
            sg = [gR.enter_context(nc.sbuf_tensor(f"f{app}_sg{i}", [128, 1024], BF16, side="right"))
                  for i in range(2)]
            ub = [gR.enter_context(nc.sbuf_tensor(f"f{app}_ub{i}", [128, 1024], BF16, side="right"))
                  for i in range(2)]

            st_at = stats_in
            wd_at = None
            for half in range(2):
                tiles = list(range(half * 8, half * 8 + 8))
                ph = Phase(nc, f"f{app}h{half}_tr")
                emit_lnT(ph, tiles, xTf, 1024, st_at, xtiles)
                gu_at = [None] * NHC
                def ld_gu(e, p, slot):
                    e.dma_start(wgr[slot][:].rearrange("p (c m) -> p c m", c=2),
                                dr["f_wg"][2 * p:2 * p + 2].transpose([1, 0, 2])
                                ).then_inc(sG.h, 16)
                    e.dma_start(wur[slot][:].rearrange("p (c m) -> p c m", c=2),
                                dr["f_wu"][2 * p:2 * p + 2].transpose([1, 0, 2])
                                ).then_inc(sG.h, 16)
                for p in range(2):
                    ph.on("gpsimd", lambda e, p=p: ld_gu(e, p, p % 2))
                    sG.n += 32
                    gu_at[2 * p] = sG.n
                    gu_at[2 * p + 1] = sG.n
                if half == 0:
                    def wdl(e):
                        for b in range(4):
                            e.dma_start(
                                wdT[:, b * 6144:(b + 1) * 6144].rearrange(
                                    "p (c m) -> p c m", c=6),
                                dr["f_wd"][b * 6:(b + 1) * 6].transpose([1, 0, 2])
                            ).then_inc(sG.h, 16)
                    ph.on("gpsimd", wdl); sG.n += 64
                    wd_at = sG.n
                ph.emit()

                ph = Phase(nc, f"f{app}h{half}_gu")
                ev_h, hp_h = {}, {}
                for hc in range(NHC):
                    par = hc % 2
                    slot = (hc // 2) % 2
                    scol0 = (hc % 2) * 1024
                    pg = pA[:, par * 1024:(par + 1) * 1024]
                    pu = pB[:, par * 1024:(par + 1) * 1024]
                    def mm(e, hc=hc, par=par, slot=slot, scol0=scol0, pg=pg, pu=pu,
                           g_at=gu_at[hc]):
                        e.wait_ge(sG.h, g_at)
                        if hc >= 2:
                            e.wait_ge(sA.h, ev_h[hc - 2])
                        for n in range(2):
                            for kc in range(ND):
                                e.matmul(pg[:, n * 512:(n + 1) * 512],
                                         wgr[slot][:, scol0 + kc * 128:scol0 + (kc + 1) * 128],
                                         xTf[:, kc * 1024 + n * 512:kc * 1024 + n * 512 + 512],
                                         start=(kc == 0), stop=(kc == ND - 1))
                            for kc in range(ND):
                                ins = e.matmul(pu[:, n * 512:(n + 1) * 512],
                                               wur[slot][:, scol0 + kc * 128:scol0 + (kc + 1) * 128],
                                               xTf[:, kc * 1024 + n * 512:kc * 1024 + n * 512 + 512],
                                               start=(kc == 0), stop=(kc == ND - 1))
                        ins.then_inc(sT.h, 1)
                    ph.on("tensor", mm); sT.n += 1
                    mm_at = sT.n
                    if hc % 2 == 1 and hc + 4 < NHC:
                        def pre(e, hc=hc, slot=slot, mm_at=mm_at):
                            e.wait_ge(sT.h, mm_at)
                            ld_gu(e, (hc + 4) // 2, slot)
                        ph.on("gpsimd", pre); sG.n += 32
                        gu_at[hc + 3] = sG.n
                        gu_at[hc + 4] = sG.n
                    def ev(e, hc=hc, par=par, pg=pg, pu=pu, mm_at=mm_at):
                        e.wait_ge(sT.h, mm_at)
                        if hc >= 2:
                            e.wait_ge(sV.h, hp_h[hc - 2])
                        e.activation(sg[par][:], pg, AF.Silu, scale=DQ1)
                        e.activation(ub[par][:], pu, AF.Identity)
                        e.drain().then_inc(sA.h, 1)
                    ph.on("scalar", ev); sA.n += 1
                    ev_h[hc] = sA.n
                    def hm(e, hc=hc, par=par, a_at=sA.n):
                        e.wait_ge(sA.h, a_at)
                        e.tensor_mul(hb[:, hc * 1024:(hc + 1) * 1024], sg[par][:],
                                     ub[par][:])
                        e.drain().then_inc(sV.h, 1)
                    ph.on("vector", hm); sV.n += 1
                    hp_h[hc] = sV.n
                ph.emit()

                lnspec = None
                if not last:
                    lnspec = "inline" if half == 0 else (app == 1)
                st_next = emit_out_residual(
                    f"f{app}h{half}_dn", None,
                    rhs_fn=lambda kc, n: wdT[:, kc * 1024 + n * 512:kc * 1024 + n * 512 + 512],
                    nK=NHC, brow_info=None, scale=DQ2, tiles=tiles,
                    target=("out" if last else "X"), accum=not last,
                    wtensor_at=(sG, wd_at), transpose_src=False,
                    lhsT_fn=lambda t, kc: hb[:, kc * 1024 + (t % 8) * 128:
                                             kc * 1024 + (t % 8) * 128 + 128],
                    ln_stats=lnspec)
            gR.close()
            gL.close()
            return st_next

        def emit_attn_b(stats_in, stop=99):
            gL = contextlib.ExitStack()
            Q = gL.enter_context(nc.sbuf_tensor("b_Q", [128, NT * 1024], BF16))
            K = gL.enter_context(nc.sbuf_tensor("b_K", [128, NT * 1024], BF16))
            vcat = gL.enter_context(nc.sbuf_tensor("b_vcat", [128, NT * 1040], BF16))

            gR = contextlib.ExitStack()
            xT = gR.enter_context(nc.sbuf_tensor("b_xT", [128, ND * 2048], BF16, side="right"))
            xtiles = [gR.enter_context(nc.sbuf_tensor(f"b_xt{i}", [128, 1024], BF16, side="right"))
                      for i in range(2)]
            nw = gR.enter_context(nc.sbuf_tensor("b_nwc", [128, ND], F32, side="right"))
            wq = [gR.enter_context(nc.sbuf_tensor(f"b_wq{i}", [128, ND * 512], BF16, side="right"))
                  for i in range(2)]
            ph = Phase(nc, "b_ln")
            ph.on("sync", lambda e: e.dma_start(nw[:], dr["b_nw"]).then_inc(sC.h, 16))
            sC.n += 16
            nw_at = sC.n
            st_at = stats_in
            ph.on("scalar", lambda e: e.wait_ge(sC.h, nw_at))
            emit_lnT(ph, list(range(NT)), xT, 2048, st_at, xtiles, nw=nw)
            wq_at = [None] * 6
            for nn in range(2):
                def pre0(e, nn=nn):
                    e.dma_start(wq[nn][:].rearrange("p (dc c) -> p dc c", dc=ND),
                                dr["b_wqkv"][nn].transpose([1, 0, 2])
                                ).then_inc(sG.h, 16)
                ph.on("gpsimd", pre0); sG.n += 16
                wq_at[nn] = sG.n
            ph.emit()
            if stop <= 1:
                emit_dump("d_xT", xT)
                gR.close(); gL.close()
                return

            cs = [gR.enter_context(nc.sbuf_tensor(f"b_cs{i}", [128, 2048], BF16, side="right"))
                  for i in range(2)]
            tmp = gR.enter_context(nc.sbuf_tensor("b_tmp", [128, 1024], BF16, side="right"))
            ph = Phase(nc, "b_qkv")
            cs_at = [None] * 32
            for j in range(2):
                ph.on("sync", lambda e, j=j: e.dma_start(
                    cs[j][:], dr["c_cossin"][(j % NT) * 128:(j % NT + 1) * 128, :]
                    ).then_inc(sC.h, 16))
                sC.n += 16
                cs_at[j] = sC.n
            rope_ct = [0]
            rope_sv = {}
            def emit_rope_job(jb):
                buf = Q if jb < 16 else K
                t = jb % 16
                dep = ev_q[(1, t)] if jb < 16 else ev_q[(3, t)]
                def rp(e, jb=jb, buf=buf, t=t, dep=dep, c_at=cs_at[jb]):
                    e.wait_ge(sC.h, c_at)
                    e.wait_ge(sA.h, dep)
                    bt = buf[:, t * 1024:(t + 1) * 1024]
                    swl = bt.rearrange("p (h two c) -> p h two c", h=H, two=2)
                    tw = tmp[:].rearrange("p (h two c) -> p h two c", h=H, two=2)
                    e.tensor_copy(tw[:, :, 0, :], swl[:, :, 1, :])
                    e.tensor_copy(tw[:, :, 1, :], swl[:, :, 0, :])
                    e.drain()
                    e.tensor_mul(bt, bt, cs[jb % 2][:, 0:1024])
                    e.tensor_mul(tmp[:], tmp[:], cs[jb % 2][:, 1024:2048])
                    e.drain()
                    e.tensor_add(bt, bt, tmp[:])
                    e.drain().then_inc(sV.h, 1)
                ph.on("vector", rp); sV.n += 1
                rope_sv[jb] = sV.n
                if jb + 2 < 32:
                    def pre(e, jb=jb, v_at=sV.n):
                        e.wait_ge(sV.h, v_at)
                        e.dma_start(cs[jb % 2][:],
                                    dr["c_cossin"][((jb + 2) % NT) * 128:
                                                   ((jb + 2) % NT + 1) * 128, :]
                                    ).then_inc(sC.h, 16)
                    ph.on("sync", pre); sC.n += 16
                    cs_at[jb + 2] = sC.n
            def vc_ones(e):
                e.memset(vcat[:].rearrange("p (t h c) -> p t h c", t=NT, c=65)[
                    :, :, :, 64:65], 1.0)
                e.drain().then_inc(sV.h, 1)
            ph.on("vector", vc_ones); sV.n += 1
            ev_q = {}
            for nn in range(6):
                par = nn % 2
                for t in range(NT):
                    bpar = t % 4
                    psum = pA[:, bpar * 512:(bpar + 1) * 512]
                    def mm(e, nn=nn, t=t, par=par, psum=psum, g_at=wq_at[nn]):
                        e.wait_ge(sG.h, g_at)
                        key = (nn, t - 4) if t >= 4 else (nn - 1, NT + t - 4)
                        if key in ev_q:
                            e.wait_ge(sA.h, ev_q[key])
                        for kc in range(ND):
                            ins = e.matmul(psum,
                                           xT[:, kc * 2048 + t * 128:kc * 2048 + t * 128 + 128],
                                           wq[par][:, kc * 512:(kc + 1) * 512],
                                           start=(kc == 0), stop=(kc == ND - 1))
                        ins.then_inc(sT.h, 1)
                    ph.on("tensor", mm); sT.n += 1
                    mm_at = sT.n
                    def ev(e, nn=nn, t=t, psum=psum, mm_at=mm_at):
                        e.wait_ge(sT.h, mm_at)
                        if nn < 2:
                            dv = Q[:, t * 1024 + nn * 512:t * 1024 + nn * 512 + 512]
                            sv = psum
                        elif nn < 4:
                            dv = K[:, t * 1024 + (nn - 2) * 512:t * 1024 + (nn - 2) * 512 + 512]
                            sv = psum
                        else:
                            h0 = 8 * (nn - 4)
                            dv = vcat[:].rearrange("p (u h c) -> p u h c", u=NT, c=65)[
                                :, t, h0:h0 + 8, 0:64]
                            sv = psum.rearrange("p (h c) -> p h c", c=64)
                        e.activation(dv, sv, AF.Identity)
                        e.drain().then_inc(sA.h, 1)
                    ph.on("scalar", ev); sA.n += 1
                    ev_q[(nn, t)] = sA.n
                    if nn == 2:
                        emit_rope_job(t)
                    elif nn == 4:
                        emit_rope_job(16 + t)
                if nn + 2 < 6:
                    mm_prev = sT.n
                    def pre(e, nn=nn, par=par, mm_prev=mm_prev):
                        e.wait_ge(sT.h, mm_prev)
                        e.dma_start(wq[par][:].rearrange("p (dc c) -> p dc c", dc=ND),
                                    dr["b_wqkv"][nn + 2].transpose([1, 0, 2])
                                    ).then_inc(sG.h, 16)
                    ph.on("gpsimd", pre); sG.n += 16
                    wq_at[nn + 2] = sG.n
            ph.emit()
            if stop <= 2:
                emit_dump("d_q", Q)
                emit_dump("d_k", K)
                emit_dump("d_v", vcat)
                gR.close(); gL.close()
                return
            gR.close()

            rope_done = max(rope_sv.values())
            if stop <= 3:
                emit_dump("d_q", Q)
                emit_dump("d_k", K)
                gR.close(); gL.close()
                return
            gR.close()

            gR3 = contextlib.ExitStack()
            ra = lambda n, sh, dt: gR3.enter_context(nc.sbuf_tensor(n, sh, dt, side="right"))
            qTp = ra("b_qTp", [128, 2048], BF16)
            kTp = ra("b_kTp", [128, 2048], BF16)
            ETb = [ra(f"b_ET{i}", [128, 1024], BF16) for i in range(2)]
            ETpmb = [ra(f"b_ETpm{i}", [16, 256], BF16) for i in range(2)]
            mask = ra("b_mask", [128, NSEG * 512], BF16)
            pmk = ra("b_pmk_sb", [128, H * PM], BF16)
            pmv = ra("b_pmv_sb", [16, H * 65], BF16)
            gAO = contextlib.ExitStack()
            AO = gAO.enter_context(nc.sbuf_tensor("b_AO", [128, NT * 1024], BF16))

            ph = Phase(nc, "b_sc")
            ph.on("sync", lambda e: e.dma_start(mask[:], dr["c_mask"]).then_inc(sC.h, 16))
            sC.n += 16
            mask_at = sC.n
            ph.on("gpsimd", lambda e: e.dma_start(pmk[:], dr["b_pmkT"]).then_inc(sG.h, 16))
            sG.n += 16
            ph.on("gpsimd", lambda e: e.dma_start(pmv[:], dr["b_pmv"]).then_inc(sG.h, 16))
            sG.n += 16
            pm_at = sG.n

            tr_ev, exp_done, av_b, ao_b, expm_hist = {}, {}, {}, {}, {}
            pend_rec, pend_av = [], []
            uu = -1
            for hp in range(8):
                for gi in range(4):
                    buf = Q if gi < 2 else K
                    dstb = qTp if gi < 2 else kTp
                    t0 = (gi % 2) * 8
                    u_guard = exp_done.get(hp * 16 - 2 + gi % 2)
                    def trg(e, hp=hp, gi=gi, buf=buf, t0=t0, u_guard=u_guard):
                        e.wait_ge(sC.h, IDENT_AT)
                        e.wait_ge(sV.h, rope_done)
                        key = (hp, gi - 2) if gi >= 2 else (hp - 1, gi + 2)
                        if key in tr_ev:
                            e.wait_ge(sV.h, tr_ev[key])
                        if u_guard is not None:
                            e.wait_ge(sA.h, u_guard)
                        for tl in range(8):
                            bank = (gi % 2) * 2 + tl % 2
                            off = bank * 512 + (tl // 2) * 128
                            ins = e.matmul(
                                pA[:, off:off + 128],
                                buf[:, (t0 + tl) * 1024 + hp * 128:
                                    (t0 + tl) * 1024 + hp * 128 + 128],
                                ident[:], start=True, stop=True)
                        ins.then_inc(sT.h, 1)
                    ph.on("tensor", trg); sT.n += 1
                    tg_at = sT.n
                    def trev(e, gi=gi, dstb=dstb, t0=t0, tg_at=tg_at):
                        e.wait_ge(sT.h, tg_at)
                        for bk in range(2):
                            dv = dstb[:].rearrange("p (u c) -> p u c", c=128)[
                                :, t0 + bk:t0 + 8:2, :]
                            e.tensor_copy(dv, pA[:, ((gi % 2) * 2 + bk) * 512:
                                                  ((gi % 2) * 2 + bk + 1) * 512])
                        e.drain().then_inc(sV.h, 1)
                    ph.on("vector", trev); sV.n += 1
                    tr_ev[(hp, gi)] = sV.n
                tr_at = sV.n

                for hh in range(2):
                    h = hp * 2 + hh
                    off = hh * 64
                    for s in range(NSEG):
                        for qh in range(2):
                            uu += 1
                            par = uu % 2
                            pE = pA[:, par * 1024:(par + 1) * 1024]
                            def smm(e, s=s, qh=qh, off=off, pE=pE, uu=uu, tr_at=tr_at):
                                e.wait_ge(sV.h, tr_at)
                                if uu >= 2:
                                    e.wait_ge(sA.h, exp_done[uu - 2])
                                q0 = s * 512 + qh * 256
                                for kc in range(4):
                                    ins = e.matmul(
                                        pE[:, kc * 256:(kc + 1) * 256],
                                        kTp[off:off + 64,
                                            s * 512 + kc * 128:s * 512 + kc * 128 + 128],
                                        qTp[off:off + 64, q0:q0 + 256],
                                        start=True, stop=True)
                                ins.then_inc(sT.h, 1)
                            ph.on("tensor", smm); sT.n += 1
                            s_at = sT.n
                            def pmm(e, h=h, s=s, qh=qh, off=off, uu=uu,
                                    prev=(expm_hist.get(uu - 2))):
                                e.wait_ge(sG.h, pm_at)
                                if prev is not None:
                                    e.wait_ge(sA.h, prev)
                                q0 = s * 512 + qh * 256
                                pmoff = (2 + uu % 2) * 512
                                e.matmul(pB[0:16, pmoff:pmoff + 256],
                                         pmk[off:off + 64, h * PM:(h + 1) * PM],
                                         qTp[off:off + 64, q0:q0 + 256],
                                         start=True, stop=True).then_inc(sT.h, 1)
                            ph.on("tensor", pmm); sT.n += 1
                            pm_mm = sT.n
                            if pend_av:
                                ua, af = pend_av.pop(0)
                                ph.on("tensor", af); sT.n += 1
                                av_b[ua] = sT.n
                            def ex(e, pE=pE, s_at=s_at, uu=uu):
                                e.wait_ge(sT.h, s_at)
                                if uu >= 2 and (uu - 2) in av_b:
                                    e.wait_ge(sT.h, av_b[uu - 2])
                                e.activation(ETb[uu % 2][:], pE, AF.Exp,
                                             scale=1.0 / 8.0).then_inc(sA.h, 1)
                            ph.on("scalar", ex); sA.n += 1
                            def expm(e, pm_mm=pm_mm, uu=uu):
                                e.wait_ge(sT.h, pm_mm)
                                if uu >= 2 and (uu - 2) in av_b:
                                    e.wait_ge(sT.h, av_b[uu - 2])
                                pmoff = (2 + uu % 2) * 512
                                e.activation(ETpmb[uu % 2][:],
                                             pB[0:16, pmoff:pmoff + 256],
                                             AF.Exp, scale=1.0 / 8.0)
                                e.drain().then_inc(sA.h, 1)
                            ph.on("scalar", expm); sA.n += 1
                            expm_at = sA.n
                            expm_hist[uu] = sA.n
                            exp_done[uu] = sA.n
                            if pend_rec:
                                ur, rf = pend_rec.pop(0)
                                ph.on("vector", rf); sV.n += 1
                                ao_b[ur] = sV.n
                            def mk(e, a0=expm_at, uu=uu, qh=qh):
                                e.wait_ge(sA.h, a0)
                                e.wait_ge(sC.h, mask_at)
                                ev = ETb[uu % 2][:].rearrange("p (kc q) -> p kc q", q=256)
                                mv = mask[:].rearrange("p (kc q) -> p kc q", q=512)[
                                    :, :, qh * 256:(qh + 1) * 256]
                                e.tensor_mul(ev, ev, mv)
                                e.drain().then_inc(sV.h, 1)
                            ph.on("vector", mk); sV.n += 1
                            mk_at = sV.n
                            def av(e, h=h, s=s, qh=qh, uu=uu, m_at=mk_at, epm=expm_at):
                                e.wait_ge(sV.h, m_at)
                                e.wait_ge(sA.h, epm)
                                if uu >= 2 and (uu - 2) in ao_b:
                                    e.wait_ge(sV.h, ao_b[uu - 2])
                                po = (uu % 2) * 512
                                for kc in range(4):
                                    gt = s * 4 + kc
                                    for mc in range(2):
                                        e.matmul(
                                            pB[:, po + mc * 128:po + mc * 128 + 65],
                                            ETb[uu % 2][:, kc * 256 + mc * 128:
                                                        kc * 256 + mc * 128 + 128],
                                            vcat[:, gt * 1040 + h * 65:gt * 1040 + h * 65 + 65],
                                            start=(kc == 0 and mc == 0), stop=False,
                                            skip_group_check=True)
                                for mc in range(2):
                                    ins = e.matmul(
                                        pB[:, po + mc * 128:po + mc * 128 + 65],
                                        ETpmb[uu % 2][:, mc * 128:mc * 128 + 128],
                                        pmv[:, h * 65:(h + 1) * 65],
                                        start=False, stop=(mc == 1),
                                        skip_group_check=True)
                                ins.then_inc(sT.h, 1)
                            pend_av.append((uu, av))
                            def recev(e, uu=uu, h=h, s=s, qh=qh):
                                e.wait_ge(sT.h, av_b[uu])
                                po = (uu % 2) * 512
                                rc = (uu % 2) * 4
                                for mc in range(2):
                                    e.reciprocal(rcol[:, rc + mc:rc + mc + 1],
                                                 pB[:, po + mc * 128 + 64:po + mc * 128 + 65])
                                e.drain()
                                for mc in range(2):
                                    t = s * 4 + qh * 2 + mc
                                    e.tensor_mul(
                                        AO[:, t * 1024 + h * 64:t * 1024 + h * 64 + 64],
                                        pB[:, po + mc * 128:po + mc * 128 + 64],
                                        rcol[:, rc + mc:rc + mc + 1].broadcast_to([128, 64]))
                                e.drain().then_inc(sV.h, 1)
                            pend_rec.append((uu, recev))
            while pend_av:
                ua, af = pend_av.pop(0)
                ph.on("tensor", af); sT.n += 1
                av_b[ua] = sT.n
            while pend_rec:
                ur, rf = pend_rec.pop(0)
                ph.on("vector", rf); sV.n += 1
                ao_b[ur] = sV.n
            ph.emit()
            if stop <= 4:
                emit_dump("d_ao", AO)
                gR3.close(); gAO.close(); gL.close()
                return
            gR3.close()

            gR4 = contextlib.ExitStack()
            wres = gR4.enter_context(nc.sbuf_tensor("b_wro", [128, ND * 1024], BF16, side="right"))
            bro = gR4.enter_context(nc.sbuf_tensor("b_bro", [1, D], BF16, side="right"))
            ph = Phase(nc, "b_wo")
            ph.on("gpsimd", lambda e: e.dma_start(bro[:], dr["b_bout"]).then_inc(sG.h, 16))
            sG.n += 16
            bro_at = sG.n
            def wo(e):
                for kc in range(ND):
                    e.dma_start(wres[:, kc * 1024:(kc + 1) * 1024],
                                dr["b_wout"][kc]).then_inc(sG.h, 16)
            ph.on("gpsimd", wo); sG.n += 16 * ND
            wo_at = sG.n
            ph.emit()
            st_ret = emit_out_residual(
                "b_out", AO,
                rhs_fn=lambda kc, n: wres[:, kc * 1024 + n * 512:kc * 1024 + n * 512 + 512],
                nK=ND, brow_info=(bro, sG, bro_at), scale=1.0,
                tiles=list(range(NT)), target="X", accum=True,
                wtensor_at=(sG, wo_at), ln_stats=False)
            gR4.close()
            gAO.close()
            gL.close()
            return st_ret

        def emit_dbg(i):
            if not debug:
                return
            ph = Phase(nc, f"dbg{i}")
            tgt = sO.n + 16
            def d(e, tgt=tgt):
                e.dma_start(dbg[i].rearrange("(t p) d -> p t d", p=128),
                            X[:].rearrange("p (t d) -> p t d", t=NT)).then_inc(sO.h, 16)
                e.wait_ge(sO.h, tgt)
            ph.on("gpsimd", d); sO.n += 16
            ph.emit()

        bstop = 99
        if isinstance(n_sub, tuple):
            n_sub, bstop = n_sub
        if n_sub == 0:
            emit_attn_a(only_ln=True)
        else:
            st = emit_attn_a()
        emit_dbg(1)
        if n_sub >= 2:
            st = emit_ffn(1, last=(n_sub == 2), stats_in=st)
            emit_dbg(2)
        if n_sub >= 3:
            st = emit_attn_b(st, stop=bstop)
            emit_dbg(3)
        if n_sub >= 4:
            emit_ffn(2, last=True, stats_in=st)

        ph = Phase(nc, "fin")
        if n_sub == 0:
            fin0 = sO.n
            ph.on("sync", lambda e: e.wait_ge(sO.h, fin0))
        elif n_sub in (2, 4):
            fin = sO.n
            ph.on("sync", lambda e: e.wait_ge(sO.h, fin))
        else:
            fin2 = sO.n + 16
            def d(e, fin2=fin2):
                e.dma_start(y.rearrange("(t p) d -> p t d", p=128),
                            X[:].rearrange("p (t d) -> p t d", t=NT)).then_inc(sO.h, 16)
                e.wait_ge(sO.h, fin2)
            ph.on("gpsimd", d); sO.n += 16
        ph.emit()

    return nc


def kernel(**inputs):
    hp = host_prep(inputs)
    xs = np.asarray(inputs["L0_attn_a_x"], np.float32)
    nc = build_program(n_sub=4, debug=False)
    in_maps = [dict(hp, x=np.ascontiguousarray(xs[b])) for b in range(B)]
    res = run_bass_kernel_spmd(nc, in_maps, core_ids=list(range(B)))
    return np.stack([res.results[b]["y"] for b in range(B)], axis=0)


# version 20
# speedup vs baseline: 1.0162x; 1.0046x over previous
import contextlib
import numpy as np
import ml_dtypes
import concourse.bass as bass
import concourse.mybir as mybir
from concourse.bass_utils import run_bass_kernel_spmd

F32 = mybir.dt.float32
BF16 = mybir.dt.bfloat16
I32 = mybir.dt.int32
AF = mybir.ActivationFunctionType
ALU = mybir.AluOpType

B, S, D = 8, 2048, 1024
H, Hd = 16, 64
HID = 3072
SEG, PM = 512, 16
NT, ND, NHC, NSEG = S // 128, D // 128, HID // 128, S // SEG

BETA = 0.95
_a4 = 1 + BETA * (1 + BETA * (1 + BETA))
TH = (1.0 / _a4, (1 + BETA) / _a4, (2 + BETA) / _a4, (2 + BETA + BETA**2) / _a4)
DQ1 = 1.0 / 127.0
DQ2 = 1.0 / (127.0 * 127.0)


def _deint_perm():
    p = []
    for h in range(H):
        p += [h * 64 + 2 * i for i in range(32)]
        p += [h * 64 + 2 * i + 1 for i in range(32)]
    return np.array(p, dtype=np.int64)


def host_prep(ins):
    c = np.ascontiguousarray
    o = {}

    def lhsT_tiles(w):
        return c(w.reshape(ND, 128, ND, 128).transpose(2, 1, 0, 3).reshape(ND, 128, D))

    o["a_qw"] = lhsT_tiles(np.asarray(ins["L0_attn_a_q_w"], np.float32))
    o["a_kw"] = lhsT_tiles(np.asarray(ins["L0_attn_a_k_w"], np.float32))
    o["a_vw"] = c(np.asarray(ins["L0_attn_a_v_w"], np.float32).reshape(ND, 128, D))
    o["a_ow"] = c(np.asarray(ins["L0_attn_a_o_w"], np.float32).reshape(ND, 128, D))
    o["a_qb"] = c(np.asarray(ins["L0_attn_a_q_b"], np.float32).reshape(ND, 128).T)
    o["a_kb"] = c(np.asarray(ins["L0_attn_a_k_b"], np.float32).reshape(ND, 128).T)
    o["a_vb"] = c(np.asarray(ins["L0_attn_a_v_b"], np.float32).reshape(1, D))
    o["a_ob"] = c(np.asarray(ins["L0_attn_a_o_b"], np.float32).reshape(1, D))

    def ffn_lhsT(w):
        return c(w.reshape(NHC, 128, ND, 128).transpose(0, 3, 2, 1).reshape(NHC, 128, D))
    o["f_wg"] = ffn_lhsT(np.asarray(ins["L0_ffn_w_gate"], np.int32))
    o["f_wu"] = ffn_lhsT(np.asarray(ins["L0_ffn_w_up"], np.int32))
    o["f_wd"] = c(np.asarray(ins["L0_ffn_w_down"], np.int32).T.reshape(NHC, 128, D))

    perm = _deint_perm()
    wqkv = np.asarray(ins["L0_attn_b_w_qkv"], np.float32).copy()
    wqkv[:, 0:D] = wqkv[:, 0:D][:, perm]
    wqkv[:, D:2 * D] = wqkv[:, D:2 * D][:, perm]
    o["b_wqkv"] = c(wqkv.reshape(ND, 128, 6, 512).transpose(2, 0, 1, 3))
    o["b_nw"] = c(np.asarray(ins["L0_attn_b_norm_w"], np.float32).reshape(ND, 128).T)
    pmk = np.asarray(ins["L0_attn_b_pm_k"], np.float32)[:, :, perm[:64]]
    pmkT = pmk.transpose(2, 0, 1).reshape(64, H * PM)
    o["b_pmkT"] = c(np.concatenate([pmkT, pmkT], axis=0))
    pmv = np.asarray(ins["L0_attn_b_pm_v"], np.float32)
    pv = np.ones((PM, H, 65), np.float32)
    pv[:, :, 0:64] = pmv.transpose(1, 0, 2)
    o["b_pmv"] = c(pv.reshape(PM, H * 65))
    o["b_wout"] = c(np.asarray(ins["L0_attn_b_w_out"], np.float32).reshape(ND, 128, D))
    o["b_bout"] = c(np.asarray(ins["L0_attn_b_b_out"], np.float32).reshape(1, D))

    o["c_ident"] = np.eye(128, dtype=ml_dtypes.bfloat16)
    m = np.zeros((128, NSEG * 512), np.float32)
    for kc in range(4):
        q = np.arange(512)[None, :]
        p = np.arange(128)[:, None]
        m[:, kc * 512:(kc + 1) * 512] = (q >= kc * 128 + p)
    o["c_mask"] = m.astype(ml_dtypes.bfloat16)
    inv = (1.0 / 10000.0 ** (np.arange(0, Hd, 2, dtype=np.float32) / Hd)).astype(np.float32)
    fr = np.arange(S, dtype=np.float32)[:, None] * inv[None, :]
    cos32, sin32 = np.cos(fr), np.sin(fr)
    cosh = np.concatenate([cos32, cos32], axis=1)
    sinh = np.concatenate([-sin32, sin32], axis=1)
    o["c_cossin"] = np.concatenate(
        [np.tile(cosh, (1, H)), np.tile(sinh, (1, H))], axis=1).astype(ml_dtypes.bfloat16)
    return o


class Sem:
    def __init__(self, stack, nc, name):
        self.h = stack.enter_context(nc.semaphore(name))
        self.n = 0


class Phase:
    ENGS = ("sync", "gpsimd", "tensor", "vector", "scalar")

    def __init__(self, nc, name):
        self.nc, self.name = nc, name
        self.ops = {e: [] for e in self.ENGS}

    def on(self, eng, fn):
        self.ops[eng].append(fn)

    def emit(self):
        with self.nc.Block(self.name) as blk:
            for eng in self.ENGS:
                fns = self.ops[eng]
                if fns:
                    getattr(blk, eng)(lambda e, fns=fns: [f(e) for f in fns])


def build_program(n_sub=4, debug=False):
    nc = bass.Bass("TRN2", target_bir_lowering=False)
    gs = contextlib.ExitStack()

    dr = {}
    def din(name, shape, dt):
        dr[name] = nc.dram_tensor(name, shape, dt, kind="ExternalInput").ap()
    din("x", [S, D], F32)
    din("a_qw", [ND, 128, D], F32); din("a_kw", [ND, 128, D], F32)
    din("a_vw", [ND, 128, D], F32); din("a_ow", [ND, 128, D], F32)
    din("a_qb", [128, ND], F32); din("a_kb", [128, ND], F32)
    din("a_vb", [1, D], F32); din("a_ob", [1, D], F32)
    din("f_wg", [NHC, 128, D], I32); din("f_wu", [NHC, 128, D], I32)
    din("f_wd", [NHC, 128, D], I32)
    din("b_wqkv", [6, ND, 128, 512], F32)
    din("b_nw", [128, ND], F32)
    din("b_pmkT", [128, H * PM], F32)
    din("b_pmv", [PM, H * 65], F32)
    din("b_wout", [ND, 128, D], F32)
    din("b_bout", [1, D], F32)
    din("c_ident", [128, 128], BF16)
    din("c_mask", [128, NSEG * 512], BF16)
    din("c_cossin", [S, 2048], BF16)
    y = nc.dram_tensor("y", [S, D], F32, kind="ExternalOutput").ap()
    dbg = {}
    if debug:
        for i in range(1, 4):
            dbg[i] = nc.dram_tensor(f"dbg{i}", [S, D], F32, kind="ExternalOutput").ap()
        for nm, cols in (("d_xT", ND * 2048), ("d_q", ND * 2048), ("d_k", ND * 2048),
                         ("d_v", NT * 1040), ("d_ao", NT * 1024)):
            dbg[nm] = nc.dram_tensor(nm, [128, cols], F32, kind="ExternalOutput").ap()

    with gs:
        sX = Sem(gs, nc, "sX"); sC = Sem(gs, nc, "sC"); sG = Sem(gs, nc, "sG")
        sT = Sem(gs, nc, "sT"); sV = Sem(gs, nc, "sV"); sA = Sem(gs, nc, "sA")
        sO = Sem(gs, nc, "sO")

        def ga(name, shape, dt, side="left"):
            return gs.enter_context(nc.sbuf_tensor(name, shape, dt, side=side))

        X = ga("X", [128, NT * 1024], BF16)
        junk = ga("junk", [128, 1024], BF16)
        ident = ga("ident", [128, 128], BF16)
        onesr = ga("onesr", [1, 128], BF16)
        sums = ga("sums", [128, NT], F32)
        sumsq = ga("sumsq", [128, NT], F32)
        mcol = ga("mcol", [128, NT], F32)
        vcol = ga("vcol", [128, NT], F32)
        tcol = ga("tcol", [128, NT], F32)
        scol = ga("scol", [128, NT], F32)
        bcol = ga("bcol", [128, NT], F32)
        rcol = ga("rcol", [128, 8], F32)
        neg4 = ga("neg4", [128, 1], F32)
        aot = [ga(f"aot{i}", [128, 1024], BF16) for i in range(2)]
        pA = gs.enter_context(nc.psum_tensor("pA", [128, 2048], F32))
        pB = gs.enter_context(nc.psum_tensor("pB", [128, 2048], F32))

        Xt = lambda t: X[:, t * 1024:(t + 1) * 1024]

        ph = Phase(nc, "load")
        def ld_x(e):
            for hf in range(4):
                e.dma_start(
                    X[:, hf * 4096:(hf + 1) * 4096].rearrange("p (t d) -> p t d", t=4),
                    dr["x"][hf * 512:(hf + 1) * 512].rearrange(
                        "(t p) d -> p t d", p=128)).then_inc(sX.h, 16)
        ph.on("gpsimd", ld_x); sX.n += 64
        ph.on("sync", lambda e: e.dma_start(ident[:], dr["c_ident"]).then_inc(sC.h, 16))
        sC.n += 16
        IDENT_AT = sC.n
        ph.on("vector", lambda e: e.memset(onesr[:], 1.0))
        ph.on("vector", lambda e: e.memset(neg4[:], -4.0))
        ph.emit()

        dump_ctr = [0]
        def emit_dump(nm, src_sb):
            if not debug:
                return
            dump_ctr[0] += 1
            ph = Phase(nc, f"dump{dump_ctr[0]}_" + nm)
            tgt = sO.n + 16
            def d(e, tgt=tgt):
                e.dma_start(dbg[nm], src_sb[:]).then_inc(sO.h, 16)
                e.wait_ge(sO.h, tgt)
            ph.on("gpsimd", d); sO.n += 16
            ph.emit()

        def emit_ln_stats(ph, have_sums, rms_fold, have_sumsq=False):
            eps_mul, eps_add = (1.0 + 1e-6, 1e-11) if rms_fold else (1.0, 1e-5)
            def v1(e):
                if not have_sumsq:
                    for t in range(NT):
                        if not have_sums and t % 4 == 0:
                            e.wait_ge(sX.h, 16 * (t // 4 + 1))
                        if not have_sums:
                            e.tensor_scalar(junk[:], Xt(t), 0.0, None, ALU.add, ALU.add,
                                            accum_out=sums[:, t:t + 1])
                        e.scalar_tensor_tensor(junk[:], Xt(t), 1.0, Xt(t), ALU.mult,
                                               ALU.mult, accum_out=sumsq[:, t:t + 1])
                e.drain()
                e.tensor_scalar(mcol[:], sums[:], 1.0 / D, None, ALU.mult)
                e.tensor_scalar(vcol[:], sumsq[:], 1.0 / D, None, ALU.mult)
                e.drain()
                e.scalar_tensor_tensor(tcol[:], mcol[:], 1.0, mcol[:], ALU.mult, ALU.mult)
                e.drain()
                e.tensor_sub(vcol[:], vcol[:], tcol[:])
                e.drain()
                e.tensor_scalar(vcol[:], vcol[:], eps_mul, eps_add, ALU.mult,
                                ALU.add)
                e.drain().then_inc(sV.h, 1)
            ph.on("vector", v1); sV.n += 1
            v1_at = sV.n
            def s1(e, v1_at=v1_at):
                e.wait_ge(sV.h, v1_at)
                e.activation(vcol[:], vcol[:], AF.Sqrt)
                e.drain().then_inc(sA.h, 1)
            ph.on("scalar", s1); sA.n += 1
            s1_at = sA.n
            def v2(e, s1_at=s1_at):
                e.wait_ge(sA.h, s1_at)
                e.reciprocal(scol[:], vcol[:])
                e.tensor_scalar(bcol[:], mcol[:], -1.0, None, ALU.mult)
                e.drain()
                e.tensor_mul(bcol[:], bcol[:], scol[:])
                e.drain().then_inc(sV.h, 1)
            ph.on("vector", v2); sV.n += 1
            return sV.n

        def emit_lnT(ph, tiles, destT, dest_cols, stats_at, xtiles, nw=None):
            n = len(tiles)
            ap_at, ev_at, mm_at = {}, {}, {}

            def mk_apply(i):
                t, par = tiles[i], i % 2
                def ap(e, t=t, par=par):
                    e.wait_ge(sV.h, stats_at)
                    e.activation(xtiles[par][:], Xt(t), AF.Identity,
                                 bias=bcol[:, t:t + 1],
                                 scale=scol[:, t:t + 1])
                    e.drain().then_inc(sA.h, 1)
                ph.on("scalar", ap); sA.n += 1
                ap_at[i] = sA.n

            def mk_tmm(i):
                par = i % 2
                def tmm(e, i=i, par=par):
                    e.wait_ge(sA.h, ap_at[i])
                    e.wait_ge(sC.h, IDENT_AT)
                    if i >= 2:
                        e.wait_ge(sV.h, ev_at[i - 2])
                    for dc in range(ND):
                        bank = par * 2 + dc // 4
                        off = bank * 512 + (dc % 4) * 128
                        ins = e.matmul(pB[:, off:off + 128],
                                       xtiles[par][:, dc * 128:(dc + 1) * 128],
                                       ident[:], start=True, stop=True)
                    ins.then_inc(sT.h, 1)
                ph.on("tensor", tmm); sT.n += 1
                mm_at[i] = sT.n

            def mk_evac(i):
                par = i % 2
                def ev(e, i=i, par=par):
                    e.wait_ge(sT.h, mm_at[i])
                    if nw is None:
                        for g in range(2):
                            src = pB[:, (par * 2 + g) * 512:(par * 2 + g + 1) * 512]
                            dv = destT[:].rearrange("p (c u) -> p c u", u=dest_cols)[
                                :, g * 4:(g + 1) * 4, i * 128:i * 128 + 128]
                            e.tensor_copy(dv, src)
                    else:
                        for dc in range(ND):
                            bank = par * 2 + dc // 4
                            off = bank * 512 + (dc % 4) * 128
                            dv = destT[:, dc * dest_cols + i * 128:
                                       dc * dest_cols + i * 128 + 128]
                            e.tensor_mul(dv, pB[:, off:off + 128],
                                         nw[:, dc:dc + 1].broadcast_to([128, 128]))
                    e.drain().then_inc(sV.h, 1)
                ph.on("vector", ev); sV.n += 1
                ev_at[i] = sV.n

            mk_apply(0)
            if n > 1:
                mk_apply(1)
            for i in range(n):
                mk_tmm(i)
                mk_evac(i)
                if i + 2 < n:
                    mk_apply(i + 2)
            return sA.n

        def emit_out_residual(phname, src_AO, rhs_fn, nK, brow_info, scale, tiles,
                              target, accum, wtensor_at, transpose_src=True,
                              lhsT_fn=None, ln_stats=None, prelude=None):
            ph = Phase(nc, phname)
            if prelude:
                prelude(ph)
            ev_hist, tr_ev, prj_at = {}, {}, {}
            st_at = {}
            n = len(tiles)

            def mk_tr(i):
                t, par = tiles[i], i % 2
                def tmm(e, i=i, t=t, par=par):
                    e.wait_ge(sC.h, IDENT_AT)
                    if i >= 2:
                        e.wait_ge(sA.h, tr_ev[i - 2])
                    for dc in range(ND):
                        bank = par * 2 + dc // 4
                        off = bank * 512 + (dc % 4) * 128
                        ins = e.matmul(pB[:, off:off + 128],
                                       src_AO[:, t * 1024 + dc * 128:t * 1024 + dc * 128 + 128],
                                       ident[:], start=True, stop=True)
                    ins.then_inc(sT.h, 1)
                ph.on("tensor", tmm); sT.n += 1
                mm = sT.n
                def ev(e, i=i, par=par, mm=mm):
                    e.wait_ge(sT.h, mm)
                    if i >= 2:
                        e.wait_ge(sT.h, prj_at[i - 2])
                    for g in range(2):
                        e.activation(aot[par][:, g * 512:(g + 1) * 512],
                                     pB[:, (par * 2 + g) * 512:(par * 2 + g + 1) * 512],
                                     AF.Identity)
                    e.drain().then_inc(sA.h, 1)
                ph.on("scalar", ev); sA.n += 1
                tr_ev[i] = sA.n

            def mk_prj(i):
                t, par = tiles[i], i % 2
                psum = pA[:, par * 1024:(par + 1) * 1024]
                def mm(e, i=i, t=t, par=par, psum=psum):
                    e.wait_ge(wtensor_at[0].h, wtensor_at[1])
                    if transpose_src:
                        e.wait_ge(sA.h, tr_ev[i])
                    if i >= 2:
                        e.wait_ge(sV.h, ev_hist[i - 2])
                    for nn in range(2):
                        for kc in range(nK):
                            lh = (aot[par][:, kc * 128:(kc + 1) * 128]
                                  if transpose_src else lhsT_fn(t, kc))
                            ins = e.matmul(psum[:, nn * 512:(nn + 1) * 512], lh,
                                           rhs_fn(kc, nn), start=(kc == 0),
                                           stop=(kc == nK - 1 and brow_info is None))
                        if brow_info is not None:
                            bw, bsem, bat = brow_info
                            e.wait_ge(bsem.h, bat)
                            ins = e.matmul(psum[:, nn * 512:(nn + 1) * 512],
                                           onesr[:], bw[:, nn * 512:(nn + 1) * 512],
                                           start=False, stop=True)
                    ins.then_inc(sT.h, 1)
                ph.on("tensor", mm); sT.n += 1
                prj_at[i] = sT.n
                def ev(e, i=i, t=t, par=par, psum=psum, mmv=sT.n):
                    e.wait_ge(sT.h, mmv)
                    if target == "out" and i >= 2:
                        e.wait_ge(sO.h, st_at[i - 2])
                    dst = Xt(t) if target == "X" else outst_buf[
                        :, (i % 2) * 1024:(i % 2 + 1) * 1024]
                    kw = {"accum_out": sums[:, t:t + 1]} if accum else {}
                    e.scalar_tensor_tensor(dst, psum[:], scale, Xt(t),
                                           ALU.mult, ALU.add, **kw)
                    if ln_stats is not None:
                        e.drain()
                        e.scalar_tensor_tensor(junk[:], Xt(t), 1.0, Xt(t), ALU.mult,
                                               ALU.mult, accum_out=sumsq[:, t:t + 1])
                    e.drain().then_inc(sV.h, 1)
                ph.on("vector", ev); sV.n += 1
                ev_hist[i] = sV.n
                if target == "out":
                    def st(e, t=t, i=i, v_at=sV.n):
                        e.wait_ge(sV.h, v_at)
                        e.dma_start(y[t * 128:(t + 1) * 128, :],
                                    outst_buf[:, (i % 2) * 1024:(i % 2 + 1) * 1024]
                                    ).then_inc(sO.h, 16)
                    ph.on("sync", st); sO.n += 16
                    st_at[i] = sO.n

            if transpose_src:
                mk_tr(0)
                for i in range(n):
                    if i + 1 < n:
                        mk_tr(i + 1)
                    mk_prj(i)
            else:
                for i in range(n):
                    mk_prj(i)
            st_ret = None
            if ln_stats is not None and ln_stats != "inline":
                st_ret = emit_ln_stats(ph, have_sums=True, rms_fold=ln_stats,
                                       have_sumsq=True)
            ph.emit()
            return st_ret

        def emit_attn_a(only_ln=False):
            gL = contextlib.ExitStack()
            gR = contextlib.ExitStack()
            xT = gR.enter_context(nc.sbuf_tensor("a_xT", [128, ND * 2048], BF16, side="right"))
            xtiles = [gR.enter_context(nc.sbuf_tensor(f"a_xt{i}", [128, 1024], BF16, side="right"))
                      for i in range(2)]
            qaggT = gL.enter_context(nc.sbuf_tensor("qaggT", [128, ND * 2048], BF16))
            kaggT = gL.enter_context(nc.sbuf_tensor("kaggT", [128, ND * 2048], BF16))
            vcat = gL.enter_context(nc.sbuf_tensor("a_vcat", [128, NT * 1040], BF16))

            gV = contextlib.ExitStack()
            bvr = gV.enter_context(nc.sbuf_tensor("bvr", [1, D], BF16, side="right"))
            wres_v = gV.enter_context(nc.sbuf_tensor("a_wrv", [128, ND * 1024], BF16, side="right"))
            ph = Phase(nc, "a_ln1")
            st_at = emit_ln_stats(ph, have_sums=False, rms_fold=False)
            emit_lnT(ph, list(range(NT)), xT, 2048, st_at, xtiles)
            def vpre(e):
                e.dma_start(bvr[:], dr["a_vb"]).then_inc(sG.h, 16)
                for b in range(2):
                    e.dma_start(
                        wres_v[:, b * 4096:(b + 1) * 4096].rearrange(
                            "p (c m) -> p c m", c=4),
                        dr["a_vw"][b * 4:(b + 1) * 4].transpose([1, 0, 2])
                    ).then_inc(sG.h, 16)
            ph.on("gpsimd", vpre); sG.n += 48
            vload_at = sG.n
            ph.emit()
            if only_ln:
                emit_dump("d_xT", xT)
                gR.close()
                gL.close()
                return

            cbuf = [gV.enter_context(nc.sbuf_tensor(f"vcb{i}", [128, 1024], F32, side="right"))
                    for i in range(2)]
            acc = gV.enter_context(nc.sbuf_tensor("vacc", [128, 1024], F32, side="right"))
            ph = Phase(nc, "a_v")
            def vc_ones(e):
                e.memset(vcat[:].rearrange("p (t h c) -> p t h c", t=NT, c=65)[
                    :, :, :, 64:65], 1.0)
                e.drain().then_inc(sV.h, 1)
            ph.on("vector", vc_ones); sV.n += 1
            ev2, lif2 = {}, {}
            for t in range(NT):
                par = t % 2
                psum = pA[:, par * 1024:(par + 1) * 1024]
                def mm(e, t=t, psum=psum):
                    e.wait_ge(sG.h, vload_at)
                    if t >= 2:
                        e.wait_ge(sA.h, ev2[t - 2])
                    for n in range(2):
                        for kc in range(ND):
                            e.matmul(psum[:, n * 512:(n + 1) * 512],
                                     xT[:, kc * 2048 + t * 128:kc * 2048 + t * 128 + 128],
                                     wres_v[:, kc * 1024 + n * 512:kc * 1024 + n * 512 + 512],
                                     start=(kc == 0), stop=False)
                        ins = e.matmul(psum[:, n * 512:(n + 1) * 512], onesr[:],
                                       bvr[:, n * 512:(n + 1) * 512],
                                       start=False, stop=True)
                    ins.then_inc(sT.h, 1)
                ph.on("tensor", mm); sT.n += 1
                mm_at = sT.n
                def ev(e, par=par, psum=psum, mm_at=mm_at, t=t):
                    e.wait_ge(sT.h, mm_at)
                    if t >= 2:
                        e.wait_ge(sV.h, lif2[t - 2])
                    e.activation(cbuf[par][:], psum, AF.Identity)
                    e.drain().then_inc(sA.h, 1)
                ph.on("scalar", ev); sA.n += 1
                ev2[t] = sA.n
                def lif(e, t=t, par=par, a_at=sA.n):
                    e.wait_ge(sA.h, a_at)
                    cb3 = cbuf[par][:].rearrange("p (h c) -> p h c", c=64)
                    a3 = acc[:].rearrange("p (h c) -> p h c", c=64)
                    cb = cbuf[par][:]
                    a = acc[:]
                    e.tensor_scalar(a, cb, TH[0], None, ALU.is_gt)
                    e.scalar_tensor_tensor(a, cb, TH[1], a, ALU.is_gt, ALU.add)
                    e.scalar_tensor_tensor(a, cb, TH[2], a, ALU.is_gt, ALU.add)
                    dstv = vcat[:].rearrange("p (u h c) -> p u h c", u=NT, c=65)[
                        :, t, :, 0:64]
                    e.scalar_tensor_tensor(dstv, cb3, TH[3], a3, ALU.is_gt,
                                           ALU.add)
                    e.drain().then_inc(sV.h, 1)
                ph.on("vector", lif); sV.n += 1
                lif2[t] = sV.n
            ph.emit()
            gV.close()

            gQ = contextlib.ExitStack()
            wring = [gQ.enter_context(nc.sbuf_tensor(f"awr{i}", [128, 2048], BF16, side="right"))
                     for i in range(3)]
            bq = gQ.enter_context(nc.sbuf_tensor("bq", [128, ND], F32, side="right"))
            bk = gQ.enter_context(nc.sbuf_tensor("bk", [128, ND], F32, side="right"))
            cbuf = [gQ.enter_context(nc.sbuf_tensor(f"qcb{i}", [128, 1024], F32, side="right"))
                    for i in range(2)]
            acc = gQ.enter_context(nc.sbuf_tensor("qacc", [128, 1024], F32, side="right"))
            ph = Phase(nc, "a_qk")
            ph.on("sync", lambda e: e.dma_start(bq[:], dr["a_qb"]).then_inc(sC.h, 16))
            sC.n += 16
            ph.on("sync", lambda e: e.dma_start(bk[:], dr["a_kb"]).then_inc(sC.h, 16))
            sC.n += 16
            bias_at = sC.n
            slabs = [("a_qw", dc) for dc in range(ND)] + [("a_kw", dc) for dc in range(ND)]
            pairs = [("a_qw", 2 * p) for p in range(4)] + [("a_kw", 2 * p) for p in range(4)]
            pair_at = [None] * len(pairs)
            def ld_pair(e, jp, slot):
                wn, d0 = pairs[jp]
                e.dma_start(wring[slot][:].rearrange("p (c m) -> p c m", c=2),
                            dr[wn][d0:d0 + 2].transpose([1, 0, 2])).then_inc(sG.h, 16)
            for jp in range(3):
                ph.on("gpsimd", lambda e, jp=jp: ld_pair(e, jp, jp % 3))
                sG.n += 16
                pair_at[jp] = sG.n
            ev_hist, lif_hist = {}, {}
            iters = [(j, n2) for j in range(len(slabs)) for n2 in range(2)]
            for it, (j, n2) in enumerate(iters):
                wn, dc = slabs[j]
                jp, par = j // 2, it % 4
                slot = jp % 3
                scol0 = (j % 2) * 1024
                psum = (pA if par < 2 else pB)[:, (par % 2) * 1024:(par % 2 + 1) * 1024]
                dst = qaggT if wn == "a_qw" else kaggT
                bias = bq if wn == "a_qw" else bk
                def mm(e, it=it, j=j, n2=n2, slot=slot, scol0=scol0, psum=psum,
                       g_at=pair_at[jp]):
                    e.wait_ge(sG.h, g_at)
                    if it >= 2:
                        e.wait_ge(sA.h, ev_hist[it - 2])
                    for n in range(2):
                        for kc in range(ND):
                            col = n2 * 1024 + n * 512
                            ins = e.matmul(
                                psum[:, n * 512:(n + 1) * 512],
                                wring[slot][:, scol0 + kc * 128:scol0 + (kc + 1) * 128],
                                xT[:, kc * 2048 + col:kc * 2048 + col + 512],
                                start=(kc == 0), stop=(kc == ND - 1))
                    ins.then_inc(sT.h, 1)
                ph.on("tensor", mm); sT.n += 1
                mm_at = sT.n
                if n2 == 1 and j % 2 == 1 and jp + 3 < len(pairs):
                    def pre(e, jp=jp, slot=slot, mm_at=mm_at):
                        e.wait_ge(sT.h, mm_at)
                        ld_pair(e, jp + 3, slot)
                    ph.on("gpsimd", pre); sG.n += 16
                    pair_at[jp + 3] = sG.n
                def ev(e, it=it, psum=psum, bias=bias, dc=dc, mm_at=mm_at):
                    e.wait_ge(sT.h, mm_at)
                    e.wait_ge(sC.h, bias_at)
                    if it >= 2:
                        e.wait_ge(sV.h, lif_hist[it - 2])
                    e.activation(cbuf[it % 2][:], psum[:], AF.Identity,
                                 bias=bias[:, dc:dc + 1])
                    e.drain().then_inc(sA.h, 1)
                ph.on("scalar", ev); sA.n += 1
                ev_hist[it] = sA.n
                def lif(e, it=it, n2=n2, dst=dst, dc=dc, a_at=sA.n):
                    e.wait_ge(sA.h, a_at)
                    cb = cbuf[it % 2]
                    e.tensor_scalar(acc[:], cb[:], TH[0], None, ALU.is_gt)
                    e.scalar_tensor_tensor(acc[:], cb[:], TH[1], acc[:], ALU.is_gt, ALU.add)
                    e.scalar_tensor_tensor(acc[:], cb[:], TH[2], acc[:], ALU.is_gt, ALU.add)
                    e.scalar_tensor_tensor(
                        dst[:, dc * 2048 + n2 * 1024:dc * 2048 + n2 * 1024 + 1024],
                        cb[:], TH[3], acc[:], ALU.is_gt, ALU.add)
                    e.drain().then_inc(sV.h, 1)
                ph.on("vector", lif); sV.n += 1
                lif_hist[it] = sV.n
            ph.emit()
            emit_dump("d_xT", xT)
            emit_dump("d_q", qaggT)
            emit_dump("d_k", kaggT)
            emit_dump("d_v", vcat)
            gQ.close()
            gR.close()

            g4 = contextlib.ExitStack()
            AO = g4.enter_context(nc.sbuf_tensor("a_AO", [128, NT * 1024], BF16))
            bor = g4.enter_context(nc.sbuf_tensor("bor", [1, D], BF16))
            gE = contextlib.ExitStack()
            ETb = [gE.enter_context(nc.sbuf_tensor(f"a_ET{i}", [128, 16 * 512], BF16))
                   for i in range(2)]

            ph = Phase(nc, "a_sc")
            ph.on("gpsimd", lambda e: e.dma_start(bor[:], dr["a_ob"]).then_inc(sG.h, 16))
            sG.n += 16
            bor_at = sG.n

            exp_hist, av_done, ao_ev = {}, {}, {}
            pend_sc, pend_av = [], []
            GRP = [(0, 4), (4, 3), (7, 4), (11, 3), (14, 2)]
            GP = [0, 1, 0, 1, 0]
            units = [(h, qb) for h in range(H) for qb in range(4)]
            for u, (h, qb) in enumerate(units):
                hc, off = h // 2, (h % 2) * 64
                for g, (kc0, nkc) in enumerate(GRP):
                    pP = pA[:, 0:2048] if GP[g] == 0 else pB[:, 0:1536]
                    def smm(e, u=u, g=g, kc0=kc0, nkc=nkc, hc=hc, off=off, qb=qb, pP=pP):
                        key = (u, g - 2) if g >= 2 else (u - 1, 4 if g == 0 else 3)
                        if key in exp_hist:
                            e.wait_ge(sA.h, exp_hist[key])
                        for k in range(nkc):
                            kc = kc0 + k
                            ins = e.matmul(
                                pP[:, k * 512:(k + 1) * 512],
                                kaggT[off:off + 64,
                                      hc * 2048 + kc * 128:hc * 2048 + kc * 128 + 128],
                                qaggT[off:off + 64,
                                      hc * 2048 + qb * 512:hc * 2048 + qb * 512 + 512],
                                start=True, stop=True)
                        ins.then_inc(sT.h, 1)
                    ph.on("tensor", smm); sT.n += 1
                    s_at = sT.n
                    def ex(e, u=u, g=g, kc0=kc0, nkc=nkc, pP=pP, s_at=s_at):
                        e.wait_ge(sT.h, s_at)
                        if u >= 2 and (u - 2) in av_done:
                            e.wait_ge(sT.h, av_done[u - 2])
                        ins = e.activation(ETb[u % 2][:, kc0 * 512:(kc0 + nkc) * 512],
                                           pP[:, 0:nkc * 512],
                                           AF.Exp, bias=neg4[:], scale=1.0 / 128.0)
                        if g == 4:
                            e.drain()
                        ins.then_inc(sA.h, 1)
                    ph.on("scalar", ex); sA.n += 1
                    exp_hist[(u, g)] = sA.n
                exp_all = sA.n
                if pend_av:
                    ua, af = pend_av.pop(0)
                    ph.on("tensor", af); sT.n += 1
                    av_done[ua] = sT.n
                if pend_sc:
                    up, scf = pend_sc.pop(0)
                    ph.on("vector", scf); sV.n += 1
                    ao_ev[up] = sV.n
                def avmm(e, u=u, h=h, exp_all=exp_all):
                    e.wait_ge(sA.h, exp_all)
                    if u >= 1:
                        e.wait_ge(sV.h, ao_ev[u - 1])
                    for kc in range(16):
                        for mc in range(4):
                            ins = e.matmul(
                                pB[:, 1536 + mc * 128:1536 + mc * 128 + 65],
                                ETb[u % 2][:, kc * 512 + mc * 128:kc * 512 + mc * 128 + 128],
                                vcat[:, kc * 1040 + h * 65:kc * 1040 + h * 65 + 65],
                                start=(kc == 0 and mc == 0),
                                stop=(kc == 15 and mc == 3),
                                skip_group_check=True)
                    ins.then_inc(sT.h, 1)
                pend_av.append((u, avmm))
                def recev(e, u=u, h=h, qb=qb):
                    e.wait_ge(sT.h, av_done[u])
                    rc = (u % 2) * 4
                    for mc in range(4):
                        e.reciprocal(rcol[:, rc + mc:rc + mc + 1],
                                     pB[:, 1536 + mc * 128 + 64:1536 + mc * 128 + 65])
                    e.drain()
                    e.tensor_scalar(rcol[:, rc:rc + 4], rcol[:, rc:rc + 4], 0.25,
                                    None, ALU.mult)
                    e.drain()
                    for mc in range(4):
                        t = qb * 4 + mc
                        e.tensor_mul(
                            AO[:, t * 1024 + h * 64:t * 1024 + h * 64 + 64],
                            pB[:, 1536 + mc * 128:1536 + mc * 128 + 64],
                            rcol[:, rc + mc:rc + mc + 1].broadcast_to([128, 64]))
                    e.drain().then_inc(sV.h, 1)
                pend_sc.append((u, recev))
            while pend_av:
                ua, af = pend_av.pop(0)
                ph.on("tensor", af); sT.n += 1
                av_done[ua] = sT.n
            while pend_sc:
                up, scf = pend_sc.pop(0)
                ph.on("vector", scf); sV.n += 1
                ao_ev[up] = sV.n
            ph.emit()
            gE.close()
            emit_dump("d_ao", AO)

            g5 = contextlib.ExitStack()
            wres = g5.enter_context(nc.sbuf_tensor("a_wro", [128, ND * 1024], BF16))
            def oslabs(e):
                for b in range(2):
                    e.dma_start(
                        wres[:, b * 4096:(b + 1) * 4096].rearrange(
                            "p (c m) -> p c m", c=4),
                        dr["a_ow"][b * 4:(b + 1) * 4].transpose([1, 0, 2])
                    ).then_inc(sG.h, 16)
            sG.n += 32
            oload_at = sG.n
            st_ret = emit_out_residual(
                "a_out", AO,
                rhs_fn=lambda kc, n: wres[:, kc * 1024 + n * 512:kc * 1024 + n * 512 + 512],
                nK=ND, brow_info=(bor, sG, bor_at), scale=1.0,
                tiles=list(range(NT)), target="X", accum=True,
                wtensor_at=(sG, oload_at), ln_stats=False,
                prelude=lambda ph: ph.on("gpsimd", oslabs))
            g5.close()
            g4.close()
            gL.close()
            return st_ret

        def emit_ffn(app, last, stats_in):
            gL = contextlib.ExitStack()
            gR = contextlib.ExitStack()
            hb = gL.enter_context(nc.sbuf_tensor(f"f{app}_hb", [128, NHC * 1024], BF16))
            wdT = gL.enter_context(nc.sbuf_tensor(f"f{app}_wdT", [128, NHC * 1024], BF16))
            global outst_buf
            if last:
                outst_buf = gL.enter_context(nc.sbuf_tensor(f"outst{app}", [128, 2048], F32))
            xTf = gR.enter_context(nc.sbuf_tensor(f"f{app}_xT", [128, ND * 1024], BF16, side="right"))
            xtiles = [gR.enter_context(nc.sbuf_tensor(f"f{app}_xt{i}", [128, 1024], BF16, side="right"))
                      for i in range(2)]
            wgr = [gR.enter_context(nc.sbuf_tensor(f"f{app}_wg{i}", [128, 2048], BF16, side="right"))
                   for i in range(2)]
            wur = [gR.enter_context(nc.sbuf_tensor(f"f{app}_wu{i}", [128, 2048], BF16, side="right"))
                   for i in range(2)]
            sg = [gR.enter_context(nc.sbuf_tensor(f"f{app}_sg{i}", [128, 1024], BF16, side="right"))
                  for i in range(2)]
            ub = [gR.enter_context(nc.sbuf_tensor(f"f{app}_ub{i}", [128, 1024], BF16, side="right"))
                  for i in range(2)]

            st_at = stats_in
            wd_at = None
            for half in range(2):
                tiles = list(range(half * 8, half * 8 + 8))
                ph = Phase(nc, f"f{app}h{half}_tr")
                emit_lnT(ph, tiles, xTf, 1024, st_at, xtiles)
                gu_at = [None] * NHC
                def ld_gu(e, p, slot):
                    e.dma_start(wgr[slot][:].rearrange("p (c m) -> p c m", c=2),
                                dr["f_wg"][2 * p:2 * p + 2].transpose([1, 0, 2])
                                ).then_inc(sG.h, 16)
                    e.dma_start(wur[slot][:].rearrange("p (c m) -> p c m", c=2),
                                dr["f_wu"][2 * p:2 * p + 2].transpose([1, 0, 2])
                                ).then_inc(sG.h, 16)
                for p in range(2):
                    ph.on("gpsimd", lambda e, p=p: ld_gu(e, p, p % 2))
                    sG.n += 32
                    gu_at[2 * p] = sG.n
                    gu_at[2 * p + 1] = sG.n
                if half == 0:
                    def wdl(e):
                        for b in range(4):
                            e.dma_start(
                                wdT[:, b * 6144:(b + 1) * 6144].rearrange(
                                    "p (c m) -> p c m", c=6),
                                dr["f_wd"][b * 6:(b + 1) * 6].transpose([1, 0, 2])
                            ).then_inc(sG.h, 16)
                    ph.on("gpsimd", wdl); sG.n += 64
                    wd_at = sG.n
                ph.emit()

                ph = Phase(nc, f"f{app}h{half}_gu")
                ev_h, hp_h = {}, {}
                for hc in range(NHC):
                    par = hc % 2
                    slot = (hc // 2) % 2
                    scol0 = (hc % 2) * 1024
                    pg = pA[:, par * 1024:(par + 1) * 1024]
                    pu = pB[:, par * 1024:(par + 1) * 1024]
                    def mm(e, hc=hc, par=par, slot=slot, scol0=scol0, pg=pg, pu=pu,
                           g_at=gu_at[hc]):
                        e.wait_ge(sG.h, g_at)
                        if hc >= 2:
                            e.wait_ge(sA.h, ev_h[hc - 2])
                        for n in range(2):
                            for kc in range(ND):
                                e.matmul(pg[:, n * 512:(n + 1) * 512],
                                         wgr[slot][:, scol0 + kc * 128:scol0 + (kc + 1) * 128],
                                         xTf[:, kc * 1024 + n * 512:kc * 1024 + n * 512 + 512],
                                         start=(kc == 0), stop=(kc == ND - 1))
                            for kc in range(ND):
                                ins = e.matmul(pu[:, n * 512:(n + 1) * 512],
                                               wur[slot][:, scol0 + kc * 128:scol0 + (kc + 1) * 128],
                                               xTf[:, kc * 1024 + n * 512:kc * 1024 + n * 512 + 512],
                                               start=(kc == 0), stop=(kc == ND - 1))
                        ins.then_inc(sT.h, 1)
                    ph.on("tensor", mm); sT.n += 1
                    mm_at = sT.n
                    if hc % 2 == 1 and hc + 4 < NHC:
                        def pre(e, hc=hc, slot=slot, mm_at=mm_at):
                            e.wait_ge(sT.h, mm_at)
                            ld_gu(e, (hc + 4) // 2, slot)
                        ph.on("gpsimd", pre); sG.n += 32
                        gu_at[hc + 3] = sG.n
                        gu_at[hc + 4] = sG.n
                    def ev(e, hc=hc, par=par, pg=pg, pu=pu, mm_at=mm_at):
                        e.wait_ge(sT.h, mm_at)
                        if hc >= 2:
                            e.wait_ge(sV.h, hp_h[hc - 2])
                        e.activation(sg[par][:], pg, AF.Silu, scale=DQ1)
                        e.activation(ub[par][:], pu, AF.Identity)
                        e.drain().then_inc(sA.h, 1)
                    ph.on("scalar", ev); sA.n += 1
                    ev_h[hc] = sA.n
                    def hm(e, hc=hc, par=par, a_at=sA.n):
                        e.wait_ge(sA.h, a_at)
                        e.tensor_mul(hb[:, hc * 1024:(hc + 1) * 1024], sg[par][:],
                                     ub[par][:])
                        e.drain().then_inc(sV.h, 1)
                    ph.on("vector", hm); sV.n += 1
                    hp_h[hc] = sV.n
                ph.emit()

                lnspec = None
                if not last:
                    lnspec = "inline" if half == 0 else (app == 1)
                st_next = emit_out_residual(
                    f"f{app}h{half}_dn", None,
                    rhs_fn=lambda kc, n: wdT[:, kc * 1024 + n * 512:kc * 1024 + n * 512 + 512],
                    nK=NHC, brow_info=None, scale=DQ2, tiles=tiles,
                    target=("out" if last else "X"), accum=not last,
                    wtensor_at=(sG, wd_at), transpose_src=False,
                    lhsT_fn=lambda t, kc: hb[:, kc * 1024 + (t % 8) * 128:
                                             kc * 1024 + (t % 8) * 128 + 128],
                    ln_stats=lnspec)
            gR.close()
            gL.close()
            return st_next

        def emit_attn_b(stats_in, stop=99):
            gL = contextlib.ExitStack()
            Q = gL.enter_context(nc.sbuf_tensor("b_Q", [128, NT * 1024], BF16))
            K = gL.enter_context(nc.sbuf_tensor("b_K", [128, NT * 1024], BF16))
            vcat = gL.enter_context(nc.sbuf_tensor("b_vcat", [128, NT * 1040], BF16))

            gR = contextlib.ExitStack()
            xT = gR.enter_context(nc.sbuf_tensor("b_xT", [128, ND * 2048], BF16, side="right"))
            xtiles = [gR.enter_context(nc.sbuf_tensor(f"b_xt{i}", [128, 1024], BF16, side="right"))
                      for i in range(2)]
            nw = gR.enter_context(nc.sbuf_tensor("b_nwc", [128, ND], F32, side="right"))
            wq = [gR.enter_context(nc.sbuf_tensor(f"b_wq{i}", [128, ND * 512], BF16, side="right"))
                  for i in range(2)]
            ph = Phase(nc, "b_ln")
            ph.on("sync", lambda e: e.dma_start(nw[:], dr["b_nw"]).then_inc(sC.h, 16))
            sC.n += 16
            nw_at = sC.n
            st_at = stats_in
            ph.on("scalar", lambda e: e.wait_ge(sC.h, nw_at))
            emit_lnT(ph, list(range(NT)), xT, 2048, st_at, xtiles, nw=nw)
            wq_at = [None] * 6
            for nn in range(2):
                def pre0(e, nn=nn):
                    e.dma_start(wq[nn][:].rearrange("p (dc c) -> p dc c", dc=ND),
                                dr["b_wqkv"][nn].transpose([1, 0, 2])
                                ).then_inc(sG.h, 16)
                ph.on("gpsimd", pre0); sG.n += 16
                wq_at[nn] = sG.n
            ph.emit()
            if stop <= 1:
                emit_dump("d_xT", xT)
                gR.close(); gL.close()
                return

            cs = [gR.enter_context(nc.sbuf_tensor(f"b_cs{i}", [128, 2048], BF16, side="right"))
                  for i in range(2)]
            tmp = gR.enter_context(nc.sbuf_tensor("b_tmp", [128, 1024], BF16, side="right"))
            ph = Phase(nc, "b_qkv")
            cs_at = [None] * 32
            for j in range(2):
                ph.on("sync", lambda e, j=j: e.dma_start(
                    cs[j][:], dr["c_cossin"][(j % NT) * 128:(j % NT + 1) * 128, :]
                    ).then_inc(sC.h, 16))
                sC.n += 16
                cs_at[j] = sC.n
            rope_ct = [0]
            rope_sv = {}
            def emit_rope_job(jb):
                buf = Q if jb < 16 else K
                t = jb % 16
                dep = ev_q[(1, t)] if jb < 16 else ev_q[(3, t)]
                def rp(e, jb=jb, buf=buf, t=t, dep=dep, c_at=cs_at[jb]):
                    e.wait_ge(sC.h, c_at)
                    e.wait_ge(sA.h, dep)
                    bt = buf[:, t * 1024:(t + 1) * 1024]
                    swl = bt.rearrange("p (h two c) -> p h two c", h=H, two=2)
                    tw = tmp[:].rearrange("p (h two c) -> p h two c", h=H, two=2)
                    e.tensor_copy(tw[:, :, 0, :], swl[:, :, 1, :])
                    e.tensor_copy(tw[:, :, 1, :], swl[:, :, 0, :])
                    e.drain()
                    e.tensor_mul(bt, bt, cs[jb % 2][:, 0:1024])
                    e.tensor_mul(tmp[:], tmp[:], cs[jb % 2][:, 1024:2048])
                    e.drain()
                    e.tensor_add(bt, bt, tmp[:])
                    e.drain().then_inc(sV.h, 1)
                ph.on("vector", rp); sV.n += 1
                rope_sv[jb] = sV.n
                if jb + 2 < 32:
                    def pre(e, jb=jb, v_at=sV.n):
                        e.wait_ge(sV.h, v_at)
                        e.dma_start(cs[jb % 2][:],
                                    dr["c_cossin"][((jb + 2) % NT) * 128:
                                                   ((jb + 2) % NT + 1) * 128, :]
                                    ).then_inc(sC.h, 16)
                    ph.on("sync", pre); sC.n += 16
                    cs_at[jb + 2] = sC.n
            def vc_ones(e):
                e.memset(vcat[:].rearrange("p (t h c) -> p t h c", t=NT, c=65)[
                    :, :, :, 64:65], 1.0)
                e.drain().then_inc(sV.h, 1)
            ph.on("vector", vc_ones); sV.n += 1
            ev_q = {}
            for nn in range(6):
                par = nn % 2
                for t in range(NT):
                    bpar = t % 4
                    psum = pA[:, bpar * 512:(bpar + 1) * 512]
                    def mm(e, nn=nn, t=t, par=par, psum=psum, g_at=wq_at[nn]):
                        e.wait_ge(sG.h, g_at)
                        key = (nn, t - 4) if t >= 4 else (nn - 1, NT + t - 4)
                        if key in ev_q:
                            e.wait_ge(sA.h, ev_q[key])
                        for kc in range(ND):
                            ins = e.matmul(psum,
                                           xT[:, kc * 2048 + t * 128:kc * 2048 + t * 128 + 128],
                                           wq[par][:, kc * 512:(kc + 1) * 512],
                                           start=(kc == 0), stop=(kc == ND - 1))
                        ins.then_inc(sT.h, 1)
                    ph.on("tensor", mm); sT.n += 1
                    mm_at = sT.n
                    def ev(e, nn=nn, t=t, psum=psum, mm_at=mm_at):
                        e.wait_ge(sT.h, mm_at)
                        if nn < 2:
                            dv = Q[:, t * 1024 + nn * 512:t * 1024 + nn * 512 + 512]
                            sv = psum
                        elif nn < 4:
                            dv = K[:, t * 1024 + (nn - 2) * 512:t * 1024 + (nn - 2) * 512 + 512]
                            sv = psum
                        else:
                            h0 = 8 * (nn - 4)
                            dv = vcat[:].rearrange("p (u h c) -> p u h c", u=NT, c=65)[
                                :, t, h0:h0 + 8, 0:64]
                            sv = psum.rearrange("p (h c) -> p h c", c=64)
                        e.activation(dv, sv, AF.Identity)
                        e.drain().then_inc(sA.h, 1)
                    ph.on("scalar", ev); sA.n += 1
                    ev_q[(nn, t)] = sA.n
                    if nn == 2:
                        emit_rope_job(t)
                    elif nn == 4:
                        emit_rope_job(16 + t)
                if nn + 2 < 6:
                    mm_prev = sT.n
                    def pre(e, nn=nn, par=par, mm_prev=mm_prev):
                        e.wait_ge(sT.h, mm_prev)
                        e.dma_start(wq[par][:].rearrange("p (dc c) -> p dc c", dc=ND),
                                    dr["b_wqkv"][nn + 2].transpose([1, 0, 2])
                                    ).then_inc(sG.h, 16)
                    ph.on("gpsimd", pre); sG.n += 16
                    wq_at[nn + 2] = sG.n
            ph.emit()
            if stop <= 2:
                emit_dump("d_q", Q)
                emit_dump("d_k", K)
                emit_dump("d_v", vcat)
                gR.close(); gL.close()
                return
            gR.close()

            rope_done = max(rope_sv.values())
            if stop <= 3:
                emit_dump("d_q", Q)
                emit_dump("d_k", K)
                gR.close(); gL.close()
                return
            gR.close()

            gR3 = contextlib.ExitStack()
            ra = lambda n, sh, dt: gR3.enter_context(nc.sbuf_tensor(n, sh, dt, side="right"))
            qTp = ra("b_qTp", [128, 2048], BF16)
            kTp = ra("b_kTp", [128, 2048], BF16)
            ETb = [ra(f"b_ET{i}", [128, 1024], BF16) for i in range(2)]
            ETpmb = [ra(f"b_ETpm{i}", [16, 256], BF16) for i in range(2)]
            mask = ra("b_mask", [128, NSEG * 512], BF16)
            pmk = ra("b_pmk_sb", [128, H * PM], BF16)
            pmv = ra("b_pmv_sb", [16, H * 65], BF16)
            gAO = contextlib.ExitStack()
            AO = gAO.enter_context(nc.sbuf_tensor("b_AO", [128, NT * 1024], BF16))

            ph = Phase(nc, "b_sc")
            ph.on("sync", lambda e: e.dma_start(mask[:], dr["c_mask"]).then_inc(sC.h, 16))
            sC.n += 16
            mask_at = sC.n
            ph.on("gpsimd", lambda e: e.dma_start(pmk[:], dr["b_pmkT"]).then_inc(sG.h, 16))
            sG.n += 16
            ph.on("gpsimd", lambda e: e.dma_start(pmv[:], dr["b_pmv"]).then_inc(sG.h, 16))
            sG.n += 16
            pm_at = sG.n

            tr_ev, exp_done, av_b, ao_b, expm_hist = {}, {}, {}, {}, {}
            pend_rec, pend_av = [], []
            uu = -1
            for hp in range(8):
                for gi in range(4):
                    buf = Q if gi < 2 else K
                    dstb = qTp if gi < 2 else kTp
                    t0 = (gi % 2) * 8
                    u_guard = exp_done.get(hp * 16 - 2 + gi % 2)
                    def trg(e, hp=hp, gi=gi, buf=buf, t0=t0, u_guard=u_guard):
                        e.wait_ge(sC.h, IDENT_AT)
                        e.wait_ge(sV.h, rope_done)
                        key = (hp, gi - 2) if gi >= 2 else (hp - 1, gi + 2)
                        if key in tr_ev:
                            e.wait_ge(sV.h, tr_ev[key])
                        if u_guard is not None:
                            e.wait_ge(sA.h, u_guard)
                        for tl in range(8):
                            bank = (gi % 2) * 2 + tl % 2
                            off = bank * 512 + (tl // 2) * 128
                            ins = e.matmul(
                                pA[:, off:off + 128],
                                buf[:, (t0 + tl) * 1024 + hp * 128:
                                    (t0 + tl) * 1024 + hp * 128 + 128],
                                ident[:], start=True, stop=True)
                        ins.then_inc(sT.h, 1)
                    ph.on("tensor", trg); sT.n += 1
                    tg_at = sT.n
                    def trev(e, gi=gi, dstb=dstb, t0=t0, tg_at=tg_at):
                        e.wait_ge(sT.h, tg_at)
                        for bk in range(2):
                            dv = dstb[:].rearrange("p (u c) -> p u c", c=128)[
                                :, t0 + bk:t0 + 8:2, :]
                            e.tensor_copy(dv, pA[:, ((gi % 2) * 2 + bk) * 512:
                                                  ((gi % 2) * 2 + bk + 1) * 512])
                        e.drain().then_inc(sV.h, 1)
                    ph.on("vector", trev); sV.n += 1
                    tr_ev[(hp, gi)] = sV.n
                tr_at = sV.n

                for hh in range(2):
                    h = hp * 2 + hh
                    off = hh * 64
                    for s in range(NSEG):
                        for qh in range(2):
                            uu += 1
                            par = uu % 2
                            pE = pA[:, par * 1024:(par + 1) * 1024]
                            def smm(e, s=s, qh=qh, off=off, pE=pE, uu=uu, tr_at=tr_at):
                                e.wait_ge(sV.h, tr_at)
                                if uu >= 2:
                                    e.wait_ge(sA.h, exp_done[uu - 2])
                                q0 = s * 512 + qh * 256
                                for kc in range(4):
                                    ins = e.matmul(
                                        pE[:, kc * 256:(kc + 1) * 256],
                                        kTp[off:off + 64,
                                            s * 512 + kc * 128:s * 512 + kc * 128 + 128],
                                        qTp[off:off + 64, q0:q0 + 256],
                                        start=True, stop=True)
                                ins.then_inc(sT.h, 1)
                            ph.on("tensor", smm); sT.n += 1
                            s_at = sT.n
                            def pmm(e, h=h, s=s, qh=qh, off=off, uu=uu,
                                    prev=(expm_hist.get(uu - 2))):
                                e.wait_ge(sG.h, pm_at)
                                if prev is not None:
                                    e.wait_ge(sA.h, prev)
                                q0 = s * 512 + qh * 256
                                pmoff = (2 + uu % 2) * 512
                                e.matmul(pB[0:16, pmoff:pmoff + 256],
                                         pmk[off:off + 64, h * PM:(h + 1) * PM],
                                         qTp[off:off + 64, q0:q0 + 256],
                                         start=True, stop=True).then_inc(sT.h, 1)
                            ph.on("tensor", pmm); sT.n += 1
                            pm_mm = sT.n
                            if pend_av:
                                ua, af = pend_av.pop(0)
                                ph.on("tensor", af); sT.n += 1
                                av_b[ua] = sT.n
                            def ex(e, pE=pE, s_at=s_at, uu=uu):
                                e.wait_ge(sT.h, s_at)
                                if uu >= 2 and (uu - 2) in av_b:
                                    e.wait_ge(sT.h, av_b[uu - 2])
                                e.activation(ETb[uu % 2][:], pE, AF.Exp,
                                             scale=1.0 / 8.0).then_inc(sA.h, 1)
                            ph.on("scalar", ex); sA.n += 1
                            def expm(e, pm_mm=pm_mm, uu=uu):
                                e.wait_ge(sT.h, pm_mm)
                                if uu >= 2 and (uu - 2) in av_b:
                                    e.wait_ge(sT.h, av_b[uu - 2])
                                pmoff = (2 + uu % 2) * 512
                                e.activation(ETpmb[uu % 2][:],
                                             pB[0:16, pmoff:pmoff + 256],
                                             AF.Exp, scale=1.0 / 8.0)
                                e.drain().then_inc(sA.h, 1)
                            ph.on("scalar", expm); sA.n += 1
                            expm_at = sA.n
                            expm_hist[uu] = sA.n
                            exp_done[uu] = sA.n
                            if pend_rec:
                                ur, rf = pend_rec.pop(0)
                                ph.on("vector", rf); sV.n += 1
                                ao_b[ur] = sV.n
                            def mk(e, a0=expm_at, uu=uu, qh=qh):
                                e.wait_ge(sA.h, a0)
                                e.wait_ge(sC.h, mask_at)
                                ev = ETb[uu % 2][:].rearrange("p (kc q) -> p kc q", q=256)
                                mv = mask[:].rearrange("p (kc q) -> p kc q", q=512)[
                                    :, :, qh * 256:(qh + 1) * 256]
                                e.tensor_mul(ev, ev, mv)
                                e.drain().then_inc(sV.h, 1)
                            ph.on("vector", mk); sV.n += 1
                            mk_at = sV.n
                            def av(e, h=h, s=s, qh=qh, uu=uu, m_at=mk_at, epm=expm_at):
                                e.wait_ge(sV.h, m_at)
                                e.wait_ge(sA.h, epm)
                                if uu >= 2 and (uu - 2) in ao_b:
                                    e.wait_ge(sV.h, ao_b[uu - 2])
                                po = (uu % 2) * 512
                                for kc in range(4):
                                    gt = s * 4 + kc
                                    for mc in range(2):
                                        e.matmul(
                                            pB[:, po + mc * 128:po + mc * 128 + 65],
                                            ETb[uu % 2][:, kc * 256 + mc * 128:
                                                        kc * 256 + mc * 128 + 128],
                                            vcat[:, gt * 1040 + h * 65:gt * 1040 + h * 65 + 65],
                                            start=(kc == 0 and mc == 0), stop=False,
                                            skip_group_check=True)
                                for mc in range(2):
                                    ins = e.matmul(
                                        pB[:, po + mc * 128:po + mc * 128 + 65],
                                        ETpmb[uu % 2][:, mc * 128:mc * 128 + 128],
                                        pmv[:, h * 65:(h + 1) * 65],
                                        start=False, stop=(mc == 1),
                                        skip_group_check=True)
                                ins.then_inc(sT.h, 1)
                            pend_av.append((uu, av))
                            def recev(e, uu=uu, h=h, s=s, qh=qh):
                                e.wait_ge(sT.h, av_b[uu])
                                po = (uu % 2) * 512
                                rc = (uu % 2) * 4
                                for mc in range(2):
                                    e.reciprocal(rcol[:, rc + mc:rc + mc + 1],
                                                 pB[:, po + mc * 128 + 64:po + mc * 128 + 65])
                                e.drain()
                                for mc in range(2):
                                    t = s * 4 + qh * 2 + mc
                                    e.tensor_mul(
                                        AO[:, t * 1024 + h * 64:t * 1024 + h * 64 + 64],
                                        pB[:, po + mc * 128:po + mc * 128 + 64],
                                        rcol[:, rc + mc:rc + mc + 1].broadcast_to([128, 64]))
                                e.drain().then_inc(sV.h, 1)
                            pend_rec.append((uu, recev))
            while pend_av:
                ua, af = pend_av.pop(0)
                ph.on("tensor", af); sT.n += 1
                av_b[ua] = sT.n
            while pend_rec:
                ur, rf = pend_rec.pop(0)
                ph.on("vector", rf); sV.n += 1
                ao_b[ur] = sV.n
            ph.emit()
            if stop <= 4:
                emit_dump("d_ao", AO)
                gR3.close(); gAO.close(); gL.close()
                return
            gR3.close()

            gR4 = contextlib.ExitStack()
            wres = gR4.enter_context(nc.sbuf_tensor("b_wro", [128, ND * 1024], BF16, side="right"))
            bro = gR4.enter_context(nc.sbuf_tensor("b_bro", [1, D], BF16, side="right"))
            def bro_ld(e):
                e.dma_start(bro[:], dr["b_bout"]).then_inc(sG.h, 16)
            sG.n += 16
            bro_at = sG.n
            def wo(e):
                for kc in range(ND):
                    e.dma_start(wres[:, kc * 1024:(kc + 1) * 1024],
                                dr["b_wout"][kc]).then_inc(sG.h, 16)
            sG.n += 16 * ND
            wo_at = sG.n
            def b_pre(ph):
                ph.on("gpsimd", bro_ld)
                ph.on("gpsimd", wo)
            st_ret = emit_out_residual(
                "b_out", AO,
                rhs_fn=lambda kc, n: wres[:, kc * 1024 + n * 512:kc * 1024 + n * 512 + 512],
                nK=ND, brow_info=(bro, sG, bro_at), scale=1.0,
                tiles=list(range(NT)), target="X", accum=True,
                wtensor_at=(sG, wo_at), ln_stats=False,
                prelude=b_pre)
            gR4.close()
            gAO.close()
            gL.close()
            return st_ret

        def emit_dbg(i):
            if not debug:
                return
            ph = Phase(nc, f"dbg{i}")
            tgt = sO.n + 16
            def d(e, tgt=tgt):
                e.dma_start(dbg[i].rearrange("(t p) d -> p t d", p=128),
                            X[:].rearrange("p (t d) -> p t d", t=NT)).then_inc(sO.h, 16)
                e.wait_ge(sO.h, tgt)
            ph.on("gpsimd", d); sO.n += 16
            ph.emit()

        bstop = 99
        if isinstance(n_sub, tuple):
            n_sub, bstop = n_sub
        if n_sub == 0:
            emit_attn_a(only_ln=True)
        else:
            st = emit_attn_a()
        emit_dbg(1)
        if n_sub >= 2:
            st = emit_ffn(1, last=(n_sub == 2), stats_in=st)
            emit_dbg(2)
        if n_sub >= 3:
            st = emit_attn_b(st, stop=bstop)
            emit_dbg(3)
        if n_sub >= 4:
            emit_ffn(2, last=True, stats_in=st)

        ph = Phase(nc, "fin")
        if n_sub == 0:
            fin0 = sO.n
            ph.on("sync", lambda e: e.wait_ge(sO.h, fin0))
        elif n_sub in (2, 4):
            fin = sO.n
            ph.on("sync", lambda e: e.wait_ge(sO.h, fin))
        else:
            fin2 = sO.n + 16
            def d(e, fin2=fin2):
                e.dma_start(y.rearrange("(t p) d -> p t d", p=128),
                            X[:].rearrange("p (t d) -> p t d", t=NT)).then_inc(sO.h, 16)
                e.wait_ge(sO.h, fin2)
            ph.on("gpsimd", d); sO.n += 16
        ph.emit()

    return nc


def kernel(**inputs):
    hp = host_prep(inputs)
    xs = np.asarray(inputs["L0_attn_a_x"], np.float32)
    nc = build_program(n_sub=4, debug=False)
    in_maps = [dict(hp, x=np.ascontiguousarray(xs[b])) for b in range(B)]
    res = run_bass_kernel_spmd(nc, in_maps, core_ids=list(range(B)))
    return np.stack([res.results[b]["y"] for b in range(B)], axis=0)
